# version 10
# baseline (speedup 1.0000x reference)
import numpy as np
import ml_dtypes
import concourse.bass as bass
import concourse.mybir as mybir
from concourse.bass_utils import run_bass_kernel_spmd

F32 = mybir.dt.float32
BF16 = mybir.dt.bfloat16
I32 = mybir.dt.int32
AF = mybir.ActivationFunctionType
ALU = mybir.AluOpType
AX = mybir.AxisListType

MAGIC = float(np.float32(3 * 2**22))

D = 1024
DI = 2048
B, N = 4, 4096
TC = 2048
NSLAB = 16
NG = 4
SPG = 4
GT = 512
INNER = 4096
KC = 4

ENGINES = ("ten", "vec", "sca", "syn", "gps")
DMA_SEMS = ("lc", "lq", "sq", "lw", "sp", "so", "lr")
QUEUE_SEM = {"gps": "qg", "syn": "qs", "sca": "qa"}
ENG_SEMS = ("tq", "vq", "cq", "gq", "uq", "qg", "qs", "qa")
BRANCH_SEMS = ("exw", "exd", "exp")


class Walk:
    def __init__(self):
        self.steps = {e: [] for e in ENGINES}
        self.led = {}

    def emit(self, eng, fn):
        self.steps[eng].append(fn)

    def bump(self, sem, n):
        c = self.led.get(sem, 0) + n
        self.led[sem] = c
        return c

    def cnt(self, sem):
        return self.led.get(sem, 0)


class Buf:
    def __init__(self, name=""):
        self.name = name
        self.rd = []
        self.wr = []

    def note_write(self, dep):
        self.rd = [dep]
        self.wr = []

    def note_read(self, dep):
        self.wr.append(dep)


def merge_deps(deps, cur_eng):
    best = {}
    for eng, sem, cnt in deps:
        if eng == cur_eng:
            continue
        if sem is None:
            continue
        best[sem] = max(best.get(sem, 0), cnt)
    return sorted(best.items())


class Ctx:
    def __init__(self, walk):
        self.w = walk

    def step(self, eng, build, reads=(), writes=(), inc=None, incn=1):
        w = self.w
        deps = []
        for b in reads:
            deps += b.rd
        for b in writes:
            deps += b.wr + b.rd
        waits = merge_deps(deps, eng)
        if inc is not None and inc in DMA_SEMS:
            inc = QUEUE_SEM[eng]
        if inc is not None:
            tgt = w.bump(inc, incn)
            dep = (("dma" if inc in ("qg", "qs", "qa") else eng), inc, tgt)
        else:
            tgt = None
            dep = None

        drain_eng = eng in ("vec", "sca", "gps") and inc not in ("qg", "qs", "qa")

        def f(e, S, _w=waits, _b=build, _i=inc, _n=incn, _d=drain_eng):
            for s, v in _w:
                e.wait_ge(S[s], v)
            ins = _b(e, S)
            if _i is not None:
                assert ins is not None
                if _d:
                    ins = e.drain()
                ins.then_inc(S[_i], _n)

        w.emit(eng, f)
        if dep is not None:
            for b in reads:
                b.note_read(dep)
            for b in writes:
                b.note_write(dep)
        return dep



def build_nc(cfg=None):
    cfg = cfg or {}
    dump = cfg.get("dump")

    nc = bass.Bass("TRN2", target_bir_lowering=False)

    dr = {}
    for nm, shape, dt in [
        ("x_in", [TC, D], F32), ("w_in", [D, 2 * DI], F32), ("b_in", [2 * DI], F32),
        ("conv_w", [DI, KC], F32), ("conv_b", [DI], F32), ("w_bc", [DI, 2 * DI], F32),
        ("b_bc", [2 * DI], F32), ("log_dec", [DI], F32), ("w_out", [DI, D], F32),
        ("b_out_bc", [128, D], F32), ("wg", [INNER, D], F32), ("wu", [INNER, D], F32),
        ("wd", [D, INNER], F32), ("ident_in", [128, 128], BF16),
        ("iota_in", [128, GT], F32), ("parity_in", [1, 2], I32),
        ("nonce_in", [1, 2], I32),
    ]:
        dr[nm] = nc.dram_tensor(nm, shape, dt, kind="ExternalInput").ap()
    y_out = nc.dram_tensor("y_out", [TC, D], F32, kind="ExternalOutput").ap()

    sc_g = nc.dram_tensor("sc_g", [INNER, D], BF16, kind="Internal").ap()
    sc_u = nc.dram_tensor("sc_u", [INNER, D], BF16, kind="Internal").ap()
    sc_d = nc.dram_tensor("sc_d", [D, INNER], BF16, kind="Internal").ap()
    sc_dt = nc.dram_tensor("sc_dt", [INNER, D], BF16, kind="Internal").ap()
    sc_gtb = nc.dram_tensor("sc_gtb", [128, 4 * 8192], BF16, kind="Internal").ap()
    sc_utb = nc.dram_tensor("sc_utb", [128, 4 * 8192], BF16, kind="Internal").ap()
    sc_winb = nc.dram_tensor("sc_winb", [4, 128, 8192], BF16, kind="Internal").ap()
    sc_wbcb = nc.dram_tensor("sc_wbcb", [8, 128, 8192], BF16, kind="Internal").ap()
    sc_woutb = nc.dram_tensor("sc_woutb", [128, 16384], BF16, kind="Internal").ap()
    sc_state = nc.dram_tensor("sc_state", [16 * 128, TC], F32, kind="Internal").ap()
    sc_cg = nc.dram_tensor("sc_cg", [16 * 128, TC], F32, kind="Internal").ap()
    shared = nc.dram_tensor("shared", [128, 80], F32, kind="Internal",
                            addr_space="Shared").ap()

    ctxs = []

    def sbuf(name, shape, dt):
        t = nc.sbuf_tensor(name, shape, dt)
        ctxs.append(t)
        return t.__enter__()

    def psum(name, shape):
        t = nc.psum_tensor(name, shape, F32)
        ctxs.append(t)
        return t.__enter__()

    X = sbuf("X", [128, NSLAB * D], F32)
    WBA = sbuf("WBA", [128, 8192], BF16)
    WBB = sbuf("WBB", [128, 8192], BF16)
    WBC = sbuf("WBC", [128, 8192], BF16)
    WBD = sbuf("WBD", [128, 8192], BF16)
    GS = sbuf("GS", [128, SPG * INNER], BF16)
    I1T = sbuf("I1T", [128, 8 * GT], BF16)
    I2Ta = sbuf("I2Ta", [128, INNER], BF16)
    I2Tb = I2Ta
    I1 = sbuf("I1", [128, D], BF16)
    S1 = sbuf("S1", [128, 1024], F32)
    S2 = sbuf("S2", [128, 1024], F32)
    STATS = sbuf("STATS", [128, 16], F32)
    CONSTV = sbuf("CONSTV", [128, 2], F32)
    nc.const_aps.register(F32, MAGIC, CONSTV[:, 0:1])
    nc.const_aps.register(F32, 1.0, CONSTV[:, 1:2])
    UC = GS[:, 0:8192]
    SG = GS[:, 8192:16384]
    UCH = [sbuf(f"UCH{i}", [128, GT + KC], BF16) for i in range(2)]
    CVB = [sbuf(f"CVB{i}", [128, GT], BF16) for i in range(2)]
    BCH = [sbuf(f"BCH{i}", [128, GT], F32) for i in range(2)]
    STC = [sbuf(f"STC{i}", [128, GT], F32) for i in range(2)]
    CGC = [sbuf(f"CGC{i}", [128, GT], F32) for i in range(2)]
    YB = [sbuf(f"YB{i}", [128, GT], BF16) for i in range(2)]
    TAILS = sbuf("TAILS", [128, 2 * 16 * (KC - 1)], BF16)
    STCOL = sbuf("STCOL", [128, 16], F32)
    IDN = sbuf("IDN", [128, 128], BF16)
    ONES = sbuf("ONES", [128, 128], F32)
    BIN = sbuf("BIN", [128, 32], F32)
    BBC = sbuf("BBC", [128, 32], F32)
    CW = sbuf("CW", [128, 16 * KC], F32)
    CB = sbuf("CB", [128, 16], F32)
    LDC = sbuf("LDC", [128, 16], F32)
    ACOL = sbuf("ACOL", [128, 16], F32)
    SPC = sbuf("SPC", [128, 16], F32)
    NSP = sbuf("NSP", [128, 16], F32)
    WS = sbuf("WS", [128, 8], F32)
    GSC = sbuf("GSC", [128, 16], F32)
    USC = sbuf("USC", [128, 16], F32)
    DSC = sbuf("DSC", [128, 16], F32)
    IOTA = S1[:, 0:GT]
    APOW = S1[:, GT:2 * GT]
    BOUTB = S2
    PRT = sbuf("PRT", [1, 2], I32)
    NONC = sbuf("NONC", [1, 2], I32)
    FLG = sbuf("FLG", [1, 2], I32)
    CARR = sbuf("CARR", [128, 16], F32)
    LN3B = sbuf("LN3B", [128, 24], BF16)
    UTL = sbuf("UTL", [128, 48], BF16)
    MPS = sbuf("MPS", [128, 48], F32)
    ACCS = sbuf("ACCS", [128, 8], F32)

    PS = [psum(f"ps{i}", [128, 512]) for i in range(8)]

    sems = {}
    for s in DMA_SEMS + ENG_SEMS + BRANCH_SEMS:
        h = nc.semaphore(s)
        ctxs.append(h)
        sems[s] = h.__enter__()

    blk_cm = nc.Block()
    blk = blk_cm.__enter__()

    w = Walk()
    c = Ctx(w)

    env = dict(locals())
    g = EnvNS(env)
    emit_all(g)

    @blk.tensor
    def _(e):
        for f in w.steps["ten"]:
            f(e, sems)

    @blk.vector
    def _(e):
        for f in w.steps["vec"]:
            f(e, sems)

    @blk.scalar
    def _(e):
        for f in w.steps["sca"]:
            f(e, sems)

    @blk.sync
    def _(e):
        for f in w.steps["syn"]:
            f(e, sems)

    @blk.gpsimd
    def _(e):
        for f in w.steps["gps"]:
            f(e, sems)

    blk_cm.__exit__(None, None, None)
    for t in reversed(ctxs):
        t.__exit__(None, None, None)
    return nc


class EnvNS:
    def __init__(self, env):
        self.cfg = env.get("cfg")
        self.__dict__.update(env)
        self.__dict__.update(env["dr"])
        c = env["c"]
        self.step = c.step
        self.w = env["w"]
        for nm in ("GS I1T I1 I2 S1 S2 STATS TAILS STCOL WS GSC "
                   "USC DSC CARR LN3B UTL MPS ACCS").split():
            setattr(self, "b" + nm, Buf(nm))
        self.bXs = [Buf(f"X{s}") for s in range(NSLAB)]
        self.S0 = env["S2"]
        self.bS0 = self.bS2
        self.bUC = self.bGS
        self.bSG = self.bGS
        self.bAPOW = self.bS1
        self.bWB = [Buf(f"WB{i}") for i in range(4)]
        b2 = Buf("I2T")
        self.bI2T = [b2, b2]
        self.I2T = [env["I2Ta"], env["I2Ta"]]
        self.bUCH = [Buf(f"UCH{i}") for i in range(2)]
        self.bCVB = [Buf(f"CVB{i}") for i in range(2)]
        self.bBCH = [Buf(f"BCH{i}") for i in range(2)]
        self.bSTC = [Buf(f"STC{i}") for i in range(2)]
        self.bCGC = [Buf(f"CGC{i}") for i in range(2)]
        self.bYB = [Buf(f"YB{i}") for i in range(2)]
        self.bPS = [Buf(f"ps{i}") for i in range(8)]
        self.bconst = Buf("const")
        self.bSCR = {n: Buf(n) for n in
                     ("sc_g", "sc_u", "sc_d", "sc_dt", "sc_state", "sc_cg",
                      "sc_winb", "sc_wbcb", "sc_woutb")}
        self.bSCRgt = {"sc_gtb": [Buf(f"gtb{i}") for i in range(4)],
                       "sc_utb": [Buf(f"utb{i}") for i in range(4)]}
        self.bQstg = [Buf(f"qstg{i}") for i in range(8)]
        self.WBs = [env["WBA"], env["WBB"], env["WBC"], env["WBD"]]
        self.ex_w = 0
        self.ex_d = 0



def emit_all(g):
    emit_startup(g)
    dump = g.dump
    if dump == "x0":
        emit_store(g)
        return
    if dump == "dbgw":
        st = g.step
        for i, scr in enumerate([g.sc_g, g.sc_u]):
            st("gps", lambda e, S, i=i, s=scr: e.dma_start(
                out=g.X[:, i * D:(i + 1) * D], in_=s[0:128, :]),
               reads=[g.bSCR["sc_g"], g.bSCR["sc_u"]], writes=[g.bXs[0]],
               inc="lc", incn=16)
        st("gps", lambda e, S: e.dma_start(
            out=g.X.ap().rearrange("p (a c) -> p a c", a=NSLAB)[:, 2:6, :],
            in_=g.sc_d[0:128, :].rearrange("p (a c) -> p a c", a=4)),
           reads=[g.bSCR["sc_d"]], writes=[g.bXs[0]], inc="lc", incn=16)
        st("gps", lambda e, S: e.dma_start(
            out=g.X.ap().rearrange("p (a c) -> p a c", a=NSLAB)[:, 6:10, :],
            in_=g.sc_dt[0:512, :].rearrange("(a p) c -> p a c", p=128)),
           reads=[g.bSCR["sc_dt"]], writes=[g.bXs[0]], inc="lc", incn=16)
        st("vec", lambda e, S: e.tensor_copy(out=g.X[:, 10 * D:10 * D + 8],
                                             in_=g.WS[:, :]),
           reads=[g.bWS], writes=[g.bXs[0]], inc="vq")
        emit_store(g)
        return
    if dump == "dbgex":
        st = g.step
        st("vec", lambda e, S: e.memset(g.UTL[:, :], 3.0),
           writes=[g.bUTL], inc="vq")
        emit_exchange(g, "tail")
        st("vec", lambda e, S: e.tensor_copy(out=g.X[:, 1 * D:1 * D + 48],
                                             in_=g.TAILS[:, 0:48]),
           reads=[g.bTAILS], writes=[g.bXs[0]], inc="vq")
        emit_store(g)
        return
    if dump == "exf1":
        st = g.step
        emit_ffn(g, "f1")
        emit_mini_prep(g)
        st("vec", lambda e, S: e.memset(g.UTL[:, :], 3.0),
           writes=[g.bUTL], inc="vq")
        emit_exchange(g, "tail")
        st("vec", lambda e, S: e.tensor_copy(out=g.X[:, 0:48],
                                             in_=g.TAILS[:, 0:48]),
           reads=[g.bTAILS], writes=[g.bXs[0]], inc="vq")
        emit_store(g)
        return
    if dump == "dbgex2":
        st = g.step
        st("vec", lambda e, S: e.memset(g.UTL[:, :], 3.0),
           writes=[g.bUTL], inc="vq")
        st("vec", lambda e, S: e.memset(g.STCOL[:, :], 5.0),
           writes=[g.bSTCOL], inc="vq")
        emit_exchange(g, "tail")
        emit_exchange(g, "carry")
        def f(e, S):
            e.tensor_copy(out=g.X[:, 1 * D:1 * D + 48], in_=g.TAILS[:, 0:48])
            return e.tensor_copy(out=g.X[:, 2 * D:2 * D + 16], in_=g.CARR[:, :])
        st("vec", f, reads=[g.bTAILS, g.bCARR], writes=[g.bXs[0]], inc="vq")
        emit_store(g)
        return
    if dump == "dbgt":
        st = g.step
        st("gps", lambda e, S: e.dma_start(
            out=g.X.ap().rearrange("p (q f) -> p q f", q=4),
            in_=g.sc_gtb.rearrange("p (q f) -> p q f", q=4)[:, :, 0:4096]),
           reads=[g.bSCRgt["sc_gtb"][0], g.bSCRgt["sc_gtb"][1],
                  g.bSCRgt["sc_gtb"][2], g.bSCRgt["sc_gtb"][3]],
           writes=[g.bXs[0]], inc="lc", incn=16)
        emit_store(g)
        return
    if dump == "dbgq1":
        st = g.step
        emit_ln_q_common(g, 0, None, True)
        st("vec", lambda e, S: e.tensor_copy(out=g.X[:, 1 * D:2 * D],
                                             in_=g.I1[:, :]),
           reads=[g.bI1], writes=[g.bXs[0]], inc="vq")
        st("vec", lambda e, S: e.tensor_copy(out=g.X[:, 2 * D:3 * D],
                                             in_=g.S1[:, :]),
           reads=[g.bS1], writes=[g.bXs[0]], inc="vq")
        st("vec", lambda e, S: e.tensor_copy(out=g.X[:, 3 * D:4 * D],
                                             in_=g.S2[:, :]),
           reads=[g.bS2], writes=[g.bXs[0]], inc="vq")

        def f(e, S):
            e.tensor_copy(out=g.X[:, 4 * D:4 * D + 16], in_=g.STATS[:, :])
            e.tensor_copy(out=g.X[:, 4 * D + 16:4 * D + 32], in_=g.GSC[:, :])
            return e.tensor_copy(out=g.X[:, 4 * D + 32:4 * D + 48], in_=g.USC[:, :])
        st("vec", f, reads=[g.bSTATS, g.bGSC, g.bUSC], writes=[g.bXs[0]], inc="vq")
        emit_store(g)
        return
    emit_ffn(g, "f1")
    if (g.cfg or {}).get("precast"):
        emit_precast(g)
    emit_mini_prep(g)
    if dump == "x1":
        emit_store(g)
        return
    emit_mamba(g)
    if (g.cfg or {}).get("mstop") in ("winq", "conv", "g0", "g1", "g2"):
        emit_store(g)
        return
    emit_phase3(g)
    if dump == "x2":
        emit_store(g)
        return
    emit_ffn(g, "f2")
    emit_store(g)



def emit_startup(g):
    st = g.step
    dr = g.dr
    X = g.X

    def f(e, S):
        e.memset(g.CONSTV[:, 0:1], MAGIC)
        return e.memset(g.CONSTV[:, 1:2], 1.0)
    st("vec", f, writes=[g.bconst], inc="vq")

    st("gps", lambda e, S: e.dma_start(
        out=X.ap().rearrange("p (s f) -> p s f", s=NSLAB),
        in_=dr["x_in"].rearrange("(s p) f -> p s f", p=128)),
       writes=g.bXs, inc="lc", incn=16)

    def ld(out_ap, in_ap):
        st("gps", lambda e, S, o=out_ap, i=in_ap: e.dma_start(
            out=o, in_=i, allow_slow_non_contiguous=True),
           writes=[g.bconst], inc="lc", incn=16)

    ld(g.IDN[:, :], dr["ident_in"][:, :])
    ld(g.PRT[:, :], dr["parity_in"][:, :])
    ld(g.NONC[:, :], dr["nonce_in"][:, :])
    ld(g.BIN[:, :], dr["b_in"].rearrange("(m p) -> p m", p=128))
    ld(g.BBC[:, :], dr["b_bc"].rearrange("(m p) -> p m", p=128))
    ld(g.CW.ap().rearrange("p (o k) -> p o k", o=16),
       dr["conv_w"].rearrange("(o p) k -> p o k", p=128))
    ld(g.CB[:, :], dr["conv_b"].rearrange("(o p) -> p o", p=128))
    ld(g.LDC[:, :], dr["log_dec"].rearrange("(o p) -> p o", p=128))

    st("vec", lambda e, S: e.memset(g.ONES[:, :], 1.0),
       writes=[g.bconst], inc="vq")

    def f(e, S):
        e.activation(out=g.S0[:, 0:16], in_=g.LDC[:, :], func=AF.Abs)
        e.drain()
        e.activation(out=g.S0[:, 16:32], in_=g.S0[:, 0:16], func=AF.Exp, scale=-1.0)
        e.drain()
        e.activation(out=g.S0[:, 32:48], in_=g.S0[:, 16:32], func=AF.Ln, bias=1.0)
        return e.activation(out=g.S0[:, 48:64], in_=g.LDC[:, :], func=AF.Relu)
    st("sca", f, reads=[g.bconst], writes=[g.bS0], inc="cq")

    def f(e, S):
        e.tensor_add(out=g.SPC[:, :], in0=g.S0[:, 32:48], in1=g.S0[:, 48:64])
        e.drain()
        return e.tensor_scalar_mul(out=g.NSP[:, :], in0=g.SPC[:, :], scalar1=-1.0)
    st("vec", f, reads=[g.bS0], writes=[g.bconst], inc="vq")

    st("sca", lambda e, S: e.activation(out=g.ACOL[:, :], in_=g.SPC[:, :],
                                        func=AF.Exp, scale=-1.0),
       reads=[g.bconst], writes=[g.bconst], inc="cq")

    QW = [g.GS.ap().bitcast(F32)[:, 0:4096], g.GS.ap().bitcast(F32)[:, 4096:8192]]
    QT = [g.WBs[0][:, 0:4096], g.WBs[0][:, 4096:8192]]
    bQTh = [Buf("qt0"), Buf("qt1")]
    bUCF2 = Buf("ucf")
    UCF = g.WBs[1].ap().bitcast(F32)
    bQW = g.bGS
    bUCF = bUCF2
    scr_names = ["sc_g", "sc_u", "sc_d"]

    early_q1 = not (g.cfg or {}).get("no_early_q1")
    if early_q1:
        for _sg in range(SPG):
            emit_ln_q_common(g, _sg, None, True, raw_scales=True)
            emit_i1_transpose(g, _sg)

    mats = [(dr["wd"], g.sc_d, "sc_d", 8, 1, 2), (dr["wg"], g.sc_g, "sc_g", 8, 4, 0),
            (dr["wu"], g.sc_u, "sc_u", 8, 4, 1)]
    bACCs = [Buf(f"acc{i}") for i in range(3)]
    bQWh = [Buf("qw0"), Buf("qw1")]
    g.bSCRq = {"sc_g": [Buf(f"sc_g{i}") for i in range(4)],
               "sc_u": [Buf(f"sc_u{i}") for i in range(4)]}

    def ld_super(e, wap, cb, b, nsub):
        if nsub == 1:
            return e.dma_start(out=QW[b][:, 0:4096],
                               in_=wap[cb * 128:(cb + 1) * 128, :])
        return e.dma_start(
            out=QW[b].rearrange("p (a f) -> p a f", a=4),
            in_=wap[cb * 512:(cb + 1) * 512, :].rearrange("(a p) f -> p a f", p=128))

    if early_q1:
        pass

    for mi, (wap, scr, snm, nsc, nsub, wcol) in enumerate(mats):
        for cb in range(nsc):
            bb = cb % 2
            st("gps", lambda e, S, a=wap, cb=cb, b=bb, ns=nsub: ld_super(e, a, cb, b, ns),
               reads=[bQWh[bb]], writes=[bQWh[bb]], inc="lq", incn=16)
            if cb == 0:
                st("vec", lambda e, S, b=bb, mi=mi: e.reduce_sum(
                    out=g.ACCS[:, 2 * mi:2 * mi + 1], in_=QW[b][:, 0:4096],
                    axis=AX.X, apply_absolute_value=True),
                   reads=[bQWh[bb]], writes=[bACCs[mi]], inc="vq")
            else:
                def f(e, S, b=bb, mi=mi):
                    e.reduce_sum(out=g.ACCS[:, 6:7], in_=QW[b][:, 0:4096],
                                 axis=AX.X, apply_absolute_value=True)
                    e.drain()
                    return e.tensor_add(out=g.ACCS[:, 2 * mi:2 * mi + 1],
                                        in0=g.ACCS[:, 2 * mi:2 * mi + 1],
                                        in1=g.ACCS[:, 6:7])
                st("vec", f, reads=[bQWh[bb], bACCs[mi]], writes=[bACCs[mi]],
                   inc="vq")

        st("ten", lambda e, S, mi=mi: e.matmul(
            g.PS[4][:, 0:1], g.ONES[:, 0:128], g.ACCS[:, 2 * mi:2 * mi + 1],
            start=True, stop=True),
           reads=[bACCs[mi], g.bconst, g.bPS[4]], writes=[g.bPS[4]], inc="tq")

        def f(e, S, m=wcol):
            e.tensor_scalar(out=g.WS[:, m:m + 1], in0=g.PS[4][:, 0:1],
                            scalar1=1.0 / 4194304.0, scalar2=1e-5,
                            op0=ALU.mult, op1=ALU.max)
            e.drain()
            return e.reciprocal(out=g.WS[:, 4 + m:5 + m], in_=g.WS[:, m:m + 1])
        st("vec", f, reads=[g.bPS[4]], writes=[g.bWS], inc="vq")

        tscr = {"sc_g": g.sc_gtb, "sc_u": g.sc_utb}.get(snm)
        tbufs = {"sc_g": g.bSCRgt["sc_gtb"], "sc_u": g.bSCRgt["sc_utb"]}.get(snm)
        for cb in range(nsc):
            bb = cb % 2
            st("gps" if cb % 2 == 0 else "syn",
               lambda e, S, a=wap, cb=cb, b=bb, ns=nsub: ld_super(e, a, cb, b, ns),
               reads=[bQWh[bb]], writes=[bQWh[bb]], inc="lq", incn=16)

            st("sca", lambda e, S, b=bb, m=wcol: e.activation(
                out=UCF[:, 0:4096], in_=QW[b][:, 0:4096], func=AF.Identity,
                scale=g.WS[:, 4 + m:5 + m], bias=MAGIC),
               reads=[bQWh[bb], g.bWS, bUCF], writes=[bUCF], inc="cq")

            def f(e, S, b=bb):
                e.tensor_scalar(out=UCF[:, 0:4096], in0=UCF[:, 0:4096],
                                scalar1=MAGIC, scalar2=-1.0,
                                op0=ALU.subtract, op1=ALU.max)
                return e.tensor_scalar(out=QT[b][:, 0:4096], in0=UCF[:, 0:4096],
                                       scalar1=1.0, scalar2=None, op0=ALU.min)
            st("vec", f, reads=[g.bWS, bUCF, bQTh[bb]],
               writes=[bQTh[bb], bUCF], inc="vq")

            if nsub == 1:
                st("sca", lambda e, S, s=scr, cb=cb, b=bb: e.dma_start(
                    out=s[cb * 128:(cb + 1) * 128, :], in_=QT[b][:, 0:4096]),
                   reads=[bQTh[bb]], writes=[g.bSCR[snm]], inc="sq", incn=16)
                continue

            qq = cb // 2
            for fk in range(8):
                bank = 6 + (fk % 2)

                def f(e, S, b=bb, fk=fk, bank=bank):
                    ins = None
                    for a in range(4):
                        ins = e.matmul(
                            g.PS[bank][:, a * 128:(a + 1) * 128],
                            QT[b][:, a * 1024 + fk * 128:a * 1024 + fk * 128 + 128],
                            g.IDN[:, :], start=True, stop=True)
                    return ins
                st("ten", f, reads=[bQTh[bb], g.bconst, g.bPS[bank]],
                   writes=[g.bPS[bank]], inc="tq")
                stg = g.I2T[0][:, fk * 512:(fk + 1) * 512]
                bstg = g.bI2T[0] if False else g.bQstg[fk]
                st("sca", lambda e, S, bank=bank, stg=stg: e.activation(
                    out=stg, in_=g.PS[bank][:, :], func=AF.Identity),
                   reads=[g.bPS[bank], bstg], writes=[bstg], inc="cq")
                st("sca", lambda e, S, t=tscr, qq=qq, cb=cb, fk=fk, stg=stg:
                   e.dma_start(
                       out=t[:, qq * 8192 + fk * 1024 + (cb % 2) * 512:
                              qq * 8192 + fk * 1024 + (cb % 2) * 512 + 512],
                       in_=stg),
                   reads=[bstg], writes=[tbufs[qq]], inc="sq", incn=16)

        if early_q1 and snm == "sc_u":
            def f(e, S):
                e.tensor_mul(out=g.USC[:, 0:4], in0=g.GSC[:, 0:4],
                             in1=g.WS[:, 1:2].broadcast_to([128, 4]))
                e.drain()
                return e.tensor_mul(out=g.GSC[:, 0:4], in0=g.GSC[:, 0:4],
                                    in1=g.WS[:, 0:1].broadcast_to([128, 4]))
            st("vec", f, reads=[g.bGSC, g.bWS], writes=[g.bGSC, g.bUSC], inc="vq")

    for kt in range(32):
        bb = kt % 2
        st("syn", lambda e, S, k=kt, b=bb: e.dma_start(
            out=QT[b][:, 0:1024], in_=g.sc_d[:, k * 128:(k + 1) * 128],
            transpose=True),
           reads=[g.bSCR["sc_d"], bQTh[bb]], writes=[bQTh[bb]],
           inc="lw", incn=16)
        st("sca", lambda e, S, k=kt, b=bb: e.dma_start(
            out=g.sc_dt[k * 128:(k + 1) * 128, :], in_=QT[b][:, 0:1024]),
           reads=[bQTh[bb]], writes=[g.bSCR["sc_dt"]], inc="sp", incn=16)

    for bq in g.bQstg:
        for dep in bq.wr + bq.rd:
            g.bI2T[0].note_read(dep)
    for dep in bQTh[0].wr + bQTh[1].wr + bQTh[0].rd + bQTh[1].rd:
        g.bWB[0].note_read(dep)
    for dep in bUCF2.wr + bUCF2.rd:
        g.bWB[1].note_read(dep)



def emit_ln_q_common(g, s, out_bf, want_quant, raw_scales=False):
    st = g.step
    Xs = g.X[:, s * D:(s + 1) * D]

    bXs = g.bXs[s]
    st("vec", lambda e, S: e.reduce_sum(out=g.STATS[:, 0:1], in_=Xs, axis=AX.X),
       reads=[bXs, g.bSTATS], writes=[g.bSTATS], inc="vq")
    st("sca", lambda e, S: e.activation(out=g.S0[:, :], in_=Xs, func=AF.Square,
                                        accum_out=g.STATS[:, 1:2]),
       reads=[bXs, g.bS0, g.bSTATS], writes=[g.bS0, g.bSTATS], inc="cq")

    def f(e, S):
        e.tensor_scalar_mul(out=g.STATS[:, 2:3], in0=g.STATS[:, 0:1],
                            scalar1=1.0 / D)
        e.drain()
        e.tensor_mul(out=g.STATS[:, 3:4], in0=g.STATS[:, 2:3], in1=g.STATS[:, 2:3])
        e.drain()
        e.scalar_tensor_tensor(out=g.STATS[:, 4:5], in0=g.STATS[:, 1:2],
                               scalar=1.0 / D, in1=g.STATS[:, 3:4],
                               op0=ALU.mult, op1=ALU.subtract)
        e.drain()
        return e.tensor_scalar_add(out=g.STATS[:, 4:5], in0=g.STATS[:, 4:5],
                                   scalar1=1e-5)
    st("vec", f, reads=[g.bSTATS], writes=[g.bSTATS], inc="vq")
    st("sca", lambda e, S: e.activation(out=g.STATS[:, 12:13], in_=g.STATS[:, 4:5],
                                        func=AF.Sqrt),
       reads=[g.bSTATS], writes=[g.bSTATS], inc="cq")

    def f(e, S):
        e.reciprocal(out=g.STATS[:, 5:6], in_=g.STATS[:, 12:13])
        e.drain()
        return e.scalar_tensor_tensor(
            out=g.STATS[:, 6:7], in0=g.STATS[:, 2:3], scalar=-1.0,
            in1=g.STATS[:, 5:6], op0=ALU.mult, op1=ALU.mult)
    st("vec", f, reads=[g.bSTATS], writes=[g.bSTATS], inc="vq")

    if not want_quant:
        st("sca", lambda e, S: e.activation(out=g.I1[:, :], in_=Xs, func=AF.Identity,
                                            scale=g.STATS[:, 5:6],
                                            bias=g.STATS[:, 6:7]),
           reads=[bXs, g.bSTATS, g.bI1], writes=[g.bI1], inc="cq")
        return

    st("sca", lambda e, S: e.activation(out=g.S1[:, :], in_=Xs, func=AF.Identity,
                                        scale=g.STATS[:, 5:6], bias=g.STATS[:, 6:7]),
       reads=[bXs, g.bSTATS, g.bS1], writes=[g.bS1], inc="cq")

    def f(e, S):
        e.reduce_max(out=g.STATS[:, 7:8], in_=g.S1[:, :], axis=AX.X,
                     apply_absolute_value=True)
        e.drain()
        e.tensor_scalar_max(out=g.STATS[:, 8:9], in0=g.STATS[:, 7:8], scalar1=1e-5)
        e.drain()
        e.reciprocal(out=g.STATS[:, 9:10], in_=g.STATS[:, 8:9])
        e.drain()
        return e.tensor_scalar_mul(out=g.STATS[:, 10:11], in0=g.STATS[:, 9:10],
                                   scalar1=127.0)
    st("vec", f, reads=[g.bS1, g.bSTATS], writes=[g.bSTATS], inc="vq")

    st("sca", lambda e, S: e.activation(out=g.S2[:, :], in_=g.S1[:, :],
                                        func=AF.Identity, scale=g.STATS[:, 10:11],
                                        bias=MAGIC),
       reads=[g.bS1, g.bSTATS, g.bS2], writes=[g.bS2], inc="cq")
    st("vec", lambda e, S: e.tensor_scalar(out=g.I1[:, :], in0=g.S2[:, :],
                                           scalar1=MAGIC, scalar2=None,
                                           op0=ALU.subtract),
       reads=[g.bS2, g.bI1], writes=[g.bI1], inc="vq")

    if raw_scales:
        st("vec", lambda e, S, s=s: e.tensor_scalar_mul(
            out=g.GSC[:, s % 16:s % 16 + 1], in0=g.STATS[:, 8:9],
            scalar1=1.0 / 127.0),
           reads=[g.bSTATS], writes=[g.bGSC], inc="vq")
        return

    def f(e, S, s=s):
        e.scalar_tensor_tensor(out=g.GSC[:, s % 16:s % 16 + 1],
                               in0=g.STATS[:, 8:9], scalar=1.0 / 127.0,
                               in1=g.WS[:, 0:1], op0=ALU.mult, op1=ALU.mult)
        return e.scalar_tensor_tensor(out=g.USC[:, s % 16:s % 16 + 1],
                                      in0=g.STATS[:, 8:9], scalar=1.0 / 127.0,
                                      in1=g.WS[:, 1:2], op0=ALU.mult, op1=ALU.mult)
    st("vec", f, reads=[g.bSTATS, g.bWS], writes=[g.bGSC, g.bUSC], inc="vq")


def emit_i1_transpose(g, sg):
    st = g.step
    for half in range(2):
        bank = 4 + half

        def f(e, S, h=half, bank=bank):
            ins = None
            for j in range(4):
                k = h * 4 + j
                ins = e.matmul(g.PS[bank][:, j * 128:(j + 1) * 128],
                               g.I1[:, k * 128:(k + 1) * 128], g.IDN[:, :],
                               start=True, stop=True)
            return ins
        st("ten", f, reads=[g.bI1, g.bconst, g.bPS[bank]], writes=[g.bPS[bank]],
           inc="tq")

        def f(e, S, h=half, bank=bank, sg=sg):
            ins = None
            for j in range(4):
                k = h * 4 + j
                ins = e.tensor_copy(
                    out=g.I1T[:, k * GT + sg * 128:k * GT + sg * 128 + 128],
                    in_=g.PS[bank][:, j * 128:(j + 1) * 128])
            return ins
        st("vec", f, reads=[g.bPS[bank], g.bI1T], writes=[g.bI1T], inc="vq")



def emit_gu_old(g, sg, q, gb, ub, grp):
    st = g.step
    WBs = g.WBs
    for n2 in range(2):
        bank = (sg % 2) * 2 + n2

        def fmm(e, S, sg=sg, n2=n2, b=gb, bank=bank):
            ins = None
            for k in range(8):
                ins = e.matmul(
                    g.PS[bank][:, :],
                    g.I1T[:, k * GT + sg * 128:k * GT + sg * 128 + 128],
                    WBs[b][:, k * 1024 + n2 * 512:k * 1024 + n2 * 512 + 512],
                    start=(k == 0), stop=(k == 7))
            return ins
        st("ten", fmm, reads=[g.bI1T, g.bWB[gb], g.bPS[bank]],
           writes=[g.bPS[bank]], inc="tq")
        cbase = sg * INNER + q * 1024 + n2 * 512
        scol = (grp * SPG + sg) % 16
        st("sca", lambda e, S, cb=cbase, bank=bank, sc=scol: e.activation(
            out=g.GS[:, cb:cb + 512], in_=g.PS[bank][:, :], func=AF.Silu,
            scale=g.GSC[:, sc:sc + 1]),
           reads=[g.bPS[bank], g.bGSC, g.bGS], writes=[g.bGS], inc="cq")

        def fmm(e, S, sg=sg, n2=n2, b=ub, bank=bank):
            ins = None
            for k in range(8):
                ins = e.matmul(
                    g.PS[bank][:, :],
                    g.I1T[:, k * GT + sg * 128:k * GT + sg * 128 + 128],
                    WBs[b][:, k * 1024 + n2 * 512:k * 1024 + n2 * 512 + 512],
                    start=(k == 0), stop=(k == 7))
            return ins
        st("ten", fmm, reads=[g.bI1T, g.bWB[ub], g.bPS[bank]],
           writes=[g.bPS[bank]], inc="tq")
        st("vec", lambda e, S, cb=cbase, bank=bank, sc=scol: e.scalar_tensor_tensor(
            out=g.GS[:, cb:cb + 512], in0=g.PS[bank][:, :],
            scalar=g.USC[:, sc:sc + 1], in1=g.GS[:, cb:cb + 512],
            op0=ALU.mult, op1=ALU.mult),
           reads=[g.bPS[bank], g.bUSC, g.bGS], writes=[g.bGS], inc="vq")


def emit_ffn(g, tag):
    st = g.step
    WBs = g.WBs
    cfg = g.cfg or {}
    ngr = cfg.get(tag + "_groups", NG)
    q1only = cfg.get(tag + "_q1only", False)

    for grp in range(ngr):
        if not (tag == "f1" and grp == 0
                and not (g.cfg or {}).get("no_early_q1")):
            for sg in range(SPG):
                emit_ln_q_common(g, grp * SPG + sg, None, True)
                emit_i1_transpose(g, sg)
        if q1only:
            continue

        for q in range(4):
            gb, ub = (q % 2) * 2, (q % 2) * 2 + 1
            for buf, tn in ((gb, "sc_gtb"), (ub, "sc_utb")):
                tsc = g.sc_gtb if tn == "sc_gtb" else g.sc_utb
                bscr = g.bSCRgt[tn][q]
                st("syn", lambda e, S, b=buf, t=tsc, q=q: e.dma_start(
                    out=WBs[b][:, :], in_=t[:, q * 8192:(q + 1) * 8192]),
                   reads=[bscr, g.bWB[buf]], writes=[g.bWB[buf]],
                   inc="lw", incn=16)
            for sg in range(SPG):
                if (g.cfg or {}).get("old_gu"):
                    emit_gu_old(g, sg, q, gb, ub, grp)
                    continue
                b0 = 4 * (sg % 2)
                bks = [b0, b0 + 1, b0 + 2, b0 + 3]

                def fmm(e, S, sg=sg, gb=gb, ub=ub, b0=b0):
                    ins = None
                    for k in range(8):
                        lhs = g.I1T[:, k * GT + sg * 128:k * GT + sg * 128 + 128]
                        for j, b in ((0, gb), (1, gb), (2, ub), (3, ub)):
                            n2 = j % 2
                            ins = e.matmul(
                                g.PS[b0 + j][:, :], lhs,
                                WBs[b][:, k * 1024 + n2 * 512:
                                       k * 1024 + n2 * 512 + 512],
                                start=(k == 0), stop=(k == 7))
                    return ins
                st("ten", fmm,
                   reads=[g.bI1T, g.bWB[gb], g.bWB[ub]] + [g.bPS[b] for b in bks],
                   writes=[g.bPS[b] for b in bks], inc="tq")

                scol = (grp * SPG + sg) % 16
                for n2 in range(2):
                    cbase = sg * INNER + q * 1024 + n2 * 512
                    st("sca", lambda e, S, cb=cbase, bank=b0 + n2, sc=scol:
                       e.activation(
                           out=g.GS[:, cb:cb + 512],
                           in_=g.PS[bank][:, :], func=AF.Silu,
                           scale=g.GSC[:, sc:sc + 1]),
                       reads=[g.bPS[b0 + n2], g.bGSC, g.bGS], writes=[g.bGS],
                       inc="cq")
                for n2 in range(2):
                    cbase = sg * INNER + q * 1024 + n2 * 512
                    st("vec", lambda e, S, cb=cbase, bank=b0 + 2 + n2, sc=scol:
                       e.scalar_tensor_tensor(
                           out=g.GS[:, cb:cb + 512],
                           in0=g.PS[bank][:, :],
                           scalar=g.USC[:, sc:sc + 1],
                           in1=g.GS[:, cb:cb + 512],
                           op0=ALU.mult, op1=ALU.mult),
                       reads=[g.bPS[b0 + 2 + n2], g.bUSC, g.bGS], writes=[g.bGS],
                       inc="vq")

        for kt in range(32):
            buf = kt // 8
            st("syn", lambda e, S, k=kt, b=buf: e.dma_start(
                out=WBs[b][:, (k % 8) * 1024:(k % 8) * 1024 + 1024],
                in_=g.sc_dt[k * 128:(k + 1) * 128, :]),
               reads=[g.bSCR["sc_dt"], g.bWB[buf]], writes=[g.bWB[buf]],
               inc="lw", incn=16)

        for sg in range(SPG):
            s = grp * SPG + sg
            hs = g.GS[:, sg * INNER:(sg + 1) * INNER]

            def f(e, S, hs=hs):
                e.reduce_max(out=g.STATS[:, 7:8], in_=hs, axis=AX.X,
                             apply_absolute_value=True)
                e.drain()
                e.tensor_scalar_max(out=g.STATS[:, 8:9], in0=g.STATS[:, 7:8],
                                    scalar1=1e-5)
                e.drain()
                e.reciprocal(out=g.STATS[:, 9:10], in_=g.STATS[:, 8:9])
                e.drain()
                return e.tensor_scalar_mul(out=g.STATS[:, 10:11],
                                           in0=g.STATS[:, 9:10], scalar1=127.0)
            st("vec", f, reads=[g.bGS, g.bSTATS], writes=[g.bSTATS], inc="vq")
            st("vec", lambda e, S, s=s: e.scalar_tensor_tensor(
                out=g.DSC[:, s % 16:s % 16 + 1], in0=g.STATS[:, 8:9],
                scalar=0.5 / 127.0, in1=g.WS[:, 2:3], op0=ALU.mult, op1=ALU.mult),
               reads=[g.bSTATS, g.bWS], writes=[g.bDSC], inc="vq")

            tb = 0
            for qq in range(4):
                st("sca", lambda e, S, hs=hs, qq=qq: e.activation(
                    out=g.S2[:, :], in_=hs[:, qq * 1024:(qq + 1) * 1024],
                    func=AF.Identity, scale=g.STATS[:, 10:11], bias=MAGIC),
                   reads=[g.bGS, g.bSTATS, g.bS2], writes=[g.bS2], inc="cq")
                st("vec", lambda e, S: e.tensor_scalar(
                    out=g.I1[:, :], in0=g.S2[:, :],
                    scalar1=MAGIC, scalar2=None, op0=ALU.subtract),
                   reads=[g.bS2, g.bI1], writes=[g.bI1], inc="vq")
                for r in range(2):
                    bank = 4 + (r % 2)

                    def f(e, S, r=r, bank=bank):
                        ins = None
                        for j in range(4):
                            k = r * 4 + j
                            ins = e.matmul(g.PS[bank][:, j * 128:(j + 1) * 128],
                                           g.I1[:, k * 128:(k + 1) * 128],
                                           g.IDN[:, :], start=True, stop=True)
                        return ins
                    st("ten", f, reads=[g.bI1, g.bconst, g.bPS[bank]],
                       writes=[g.bPS[bank]], inc="tq")
                    st("vec", lambda e, S, r=r, bank=bank, qq=qq: e.tensor_copy(
                        out=g.I2T[0][:, qq * 1024 + r * 512:
                                    qq * 1024 + r * 512 + 512],
                        in_=g.PS[bank][:, :]),
                       reads=[g.bPS[bank], g.bI2T[0]], writes=[g.bI2T[0]],
                       inc="vq")

            for n2 in range(2):
                bank = 6 + n2

                def f(e, S, n2=n2, bank=bank, tb=tb):
                    ins = None
                    for k in range(32):
                        ins = e.matmul(
                            g.PS[bank][:, :],
                            g.I2T[tb][:, k * 128:(k + 1) * 128],
                            WBs[k // 8][:, (k % 8) * 1024 + n2 * 512:
                                        (k % 8) * 1024 + n2 * 512 + 512],
                            start=(k == 0), stop=(k == 31))
                    return ins
                st("ten", f, reads=[g.bI2T[tb], g.bWB[0], g.bWB[1], g.bWB[2],
                                    g.bWB[3], g.bPS[bank]],
                   writes=[g.bPS[bank]], inc="tq")
                st("vec", lambda e, S, s=s, n2=n2, bank=bank:
                   e.scalar_tensor_tensor(
                       out=g.X[:, s * D + n2 * 512:s * D + n2 * 512 + 512],
                       in0=g.PS[bank][:, :], scalar=g.DSC[:, s % 16:s % 16 + 1],
                       in1=g.X[:, s * D + n2 * 512:s * D + n2 * 512 + 512],
                       op0=ALU.mult, op1=ALU.add),
                   reads=[g.bPS[bank], g.bDSC, g.bXs[s]], writes=[g.bXs[s]],
                   inc="vq")



def emit_precast(g):
    st = g.step
    dr = g.dr
    for mq in range(4):
        st("gps", lambda e, S, mq=mq: e.dma_start(
            out=g.sc_winb[mq].rearrange("p (k c) -> p k c", k=8),
            in_=dr["w_in"].rearrange("(k p) c -> p k c", p=128)
                [:, :, mq * 1024:(mq + 1) * 1024]),
           writes=[g.bSCR["sc_winb"]], inc="sp", incn=16)
    for me in range(8):
        st("gps", lambda e, S, me=me: e.dma_start(
            out=g.sc_wbcb[me].rearrange("p (k c) -> p k c", k=16),
            in_=dr["w_bc"].rearrange("(k p) c -> p k c", p=128)
                [:, :, me * 512:(me + 1) * 512]),
           writes=[g.bSCR["sc_wbcb"]], inc="sp", incn=16)
    st("gps", lambda e, S: e.dma_start(
        out=g.sc_woutb.rearrange("p (k c) -> p k c", k=16),
        in_=dr["w_out"].rearrange("(k p) c -> p k c", p=128)),
       writes=[g.bSCR["sc_woutb"]], inc="sp", incn=16)


def emit_mini_prep(g):
    st = g.step
    st("gps", lambda e, S: e.dma_start(out=g.S1[0:3, :],
                                       in_=g.X[125:128, 15 * D:16 * D]),
       reads=[g.bXs[15], g.bS1], writes=[g.bS1], inc="lc", incn=16)

    def f(e, S):
        e.reduce_sum(out=g.STATS[0:3, 0:1], in_=g.S1[0:3, :], axis=AX.X)
        e.drain()
        return e.tensor_scalar_mul(out=g.STATS[0:3, 2:3], in0=g.STATS[0:3, 0:1],
                                   scalar1=1.0 / D)
    st("vec", f, reads=[g.bS1, g.bSTATS], writes=[g.bSTATS], inc="vq")
    st("sca", lambda e, S: e.activation(out=g.S0[0:3, :], in_=g.S1[0:3, :],
                                        func=AF.Square,
                                        accum_out=g.STATS[0:3, 1:2]),
       reads=[g.bS1, g.bS0, g.bSTATS], writes=[g.bS0, g.bSTATS], inc="cq")

    def f(e, S):
        e.tensor_mul(out=g.STATS[0:3, 3:4], in0=g.STATS[0:3, 2:3],
                     in1=g.STATS[0:3, 2:3])
        e.drain()
        e.scalar_tensor_tensor(out=g.STATS[0:3, 4:5],
                               in0=g.STATS[0:3, 1:2], scalar=1.0 / D,
                               in1=g.STATS[0:3, 3:4],
                               op0=ALU.mult, op1=ALU.subtract)
        e.drain()
        return e.tensor_scalar_add(out=g.STATS[0:3, 4:5],
                                   in0=g.STATS[0:3, 4:5], scalar1=1e-5)
    st("vec", f, reads=[g.bSTATS], writes=[g.bSTATS], inc="vq")
    st("sca", lambda e, S: e.activation(out=g.STATS[0:3, 12:13],
                                        in_=g.STATS[0:3, 4:5],
                                        func=AF.Sqrt),
       reads=[g.bSTATS], writes=[g.bSTATS], inc="cq")

    def f(e, S):
        e.reciprocal(out=g.STATS[0:3, 5:6], in_=g.STATS[0:3, 12:13])
        e.drain()
        return e.scalar_tensor_tensor(
            out=g.STATS[0:3, 6:7], in0=g.STATS[0:3, 2:3], scalar=-1.0,
            in1=g.STATS[0:3, 5:6], op0=ALU.mult, op1=ALU.mult)
    st("vec", f, reads=[g.bSTATS], writes=[g.bSTATS], inc="vq")
    st("sca", lambda e, S: e.activation(out=g.I1[0:3, :], in_=g.S1[0:3, :],
                                        func=AF.Identity, scale=g.STATS[0:3, 5:6],
                                        bias=g.STATS[0:3, 6:7]),
       reads=[g.bS1, g.bSTATS, g.bI1], writes=[g.bI1], inc="cq")

    for half in range(2):
        bank = 4 + half

        def f(e, S, h=half, bank=bank):
            ins = None
            for j in range(4):
                k = h * 4 + j
                ins = e.matmul(g.PS[bank][:, j * 3:(j + 1) * 3],
                               g.I1[0:3, k * 128:(k + 1) * 128], g.IDN[0:3, 0:3],
                               start=True, stop=True)
            return ins
        st("ten", f, reads=[g.bI1, g.bconst, g.bPS[bank]], writes=[g.bPS[bank]],
           inc="tq")

        def f(e, S, h=half, bank=bank):
            ins = None
            for j in range(4):
                k = h * 4 + j
                ins = e.tensor_copy(out=g.LN3B[:, k * 3:(k + 1) * 3],
                                    in_=g.PS[bank][:, j * 3:(j + 1) * 3])
            return ins
        st("vec", f, reads=[g.bPS[bank], g.bLN3B], writes=[g.bLN3B], inc="vq")



def emit_exchange(g, what):
    st = g.step
    exv = (g.cfg or {}).get("exv", 3)
    if g.dump == "noex" or (g.cfg or {}).get("noex"):
        if what == "tail":
            st("gps", lambda e, S: e.memset(g.TAILS[:, 0:48], 0.0),
               reads=[g.bUTL], writes=[g.bTAILS], inc="uq")
        else:
            st("gps", lambda e, S: e.memset(g.CARR[:, :], 0.0),
               reads=[g.bSTCOL], writes=[g.bCARR], inc="uq")
        return
    shared = g.shared
    sharedB = shared.bitcast(BF16)
    sharedI = shared.bitcast(I32)
    fcol = 78 if what == "tail" else 79
    ncol = 0 if what == "tail" else 1

    if what == "tail":
        bsrc, bdst = g.bUTL, g.bTAILS
        w_out = lambda e: e.dma_start(out=sharedB[:, 32:80], in_=g.UTL[:, :])
        r_in = lambda e: e.dma_start(out=g.TAILS[:, 0:48], in_=sharedB[:, 32:80])
        zero = lambda e: e.memset(g.TAILS[:, 0:48], 0.0)
    else:
        bsrc, bdst = g.bSTCOL, g.bCARR
        w_out = lambda e: e.dma_start(out=shared[:, 0:16], in_=g.STCOL[:, :])
        r_in = lambda e: e.dma_start(out=g.CARR[:, :], in_=shared[:, 0:16])
        zero = lambda e: e.memset(g.CARR[:, :], 0.0)

    exw0 = g.ex_w
    exd0 = g.ex_d
    g.ex_w += 32
    g.ex_d += 16

    def f(e, S, exw0=exw0, exd0=exd0, exv=exv):
        rp = e.alloc_register()
        e.reg_load(rp, g.PRT[0:1, 0:1])
        rn = e.alloc_register()
        e.reg_load(rn, g.NONC[0:1, ncol:ncol + 1])
        with e.If_eq(rp, 0):
            if exv >= 2:
                w_out(e).then_inc(S["exw"], 16)
                e.wait_ge(S["exw"], exw0 + 16)
            e.dma_start(out=sharedI[0:1, fcol:fcol + 1],
                        in_=g.NONC[0:1, ncol:ncol + 1]).then_inc(S["exw"], 16)
            e.wait_ge(S["exw"], exw0 + 32 if exv >= 2 else exw0 + 16)
            zero(e)
        e.end_ifs()
        with e.If_eq(rp, 1):
            rf = e.alloc_register()
            rd = e.alloc_register()
            e.reg_mov(rd, 1)
            with e.While(rd):
                e.dma_start(out=g.FLG[0:1, ncol:ncol + 1],
                            in_=sharedI[0:1, fcol:fcol + 1]).then_inc(S["exp"], 16)
                e.nop(cycle_cnt=2500)
                e.reg_load(rf, g.FLG[0:1, ncol:ncol + 1])
                e.reg_sub(rd, rf, rn)
            if exv >= 2:
                r_in(e).then_inc(S["exd"], 16)
                e.wait_ge(S["exd"], exd0 + 16)
            else:
                zero(e)
        e.end_ifs()
        return e.engine_nop()

    st("gps", f, reads=[bsrc, g.bconst], writes=[bdst], inc="uq")



def emit_conv_chunk(g, grp, m):
    st = g.step
    tsl_rd = (grp % 2) * 48
    tsl_wr = ((grp + 1) % 2) * 48
    cb = m % 2

    def f(e, S, m=m, cb=cb, tr=tsl_rd, tw=tsl_wr):
        e.tensor_copy(out=g.TAILS[:, tw + m * 3:tw + (m + 1) * 3],
                      in_=g.UC[:, (m + 1) * GT - 3:(m + 1) * GT])
        e.tensor_copy(out=g.UCH[cb][:, 0:3],
                      in_=g.TAILS[:, tr + m * 3:tr + (m + 1) * 3])
        return e.tensor_copy(out=g.UCH[cb][:, 3:3 + GT],
                             in_=g.UC[:, m * GT:(m + 1) * GT])
    st("vec", f, reads=[g.bUC, g.bTAILS, g.bUCH[cb]],
       writes=[g.bUCH[cb], g.bTAILS], inc="vq")

    def f(e, S, m=m, cb=cb):
        e.tensor_scalar_mul(out=g.CVB[cb][:, :], in0=g.UCH[cb][:, 0:GT],
                            scalar1=g.CW[:, m * KC:m * KC + 1])
        ins = None
        for kk in range(1, KC):
            ins = e.scalar_tensor_tensor(
                out=g.CVB[cb][:, :], in0=g.UCH[cb][:, kk:kk + GT],
                scalar=g.CW[:, m * KC + kk:m * KC + kk + 1],
                in1=g.CVB[cb][:, :], op0=ALU.mult, op1=ALU.add)
        return ins
    st("vec", f, reads=[g.bUCH[cb], g.bconst, g.bCVB[cb]],
       writes=[g.bCVB[cb]], inc="vq")

    st("sca", lambda e, S, m=m, cb=cb: e.activation(
        out=g.UC[:, m * GT:(m + 1) * GT], in_=g.CVB[cb][:, :],
        func=AF.Silu, bias=g.CB[:, m:m + 1]),
       reads=[g.bCVB[cb], g.bconst, g.bUC], writes=[g.bUC], inc="cq")


def emit_mamba(g):
    st = g.step
    WBs = g.WBs
    dr = g.dr
    mstop = (g.cfg or {}).get("mstop")

    for grp in range(NG):
        if mstop in ("g1", "g2", "g3") and grp > int(mstop[1]):
            break
        for sg in range(SPG):
            emit_ln_q_common(g, grp * SPG + sg, None, False)
            emit_i1_transpose(g, sg)

        for mq in range(4):
            buf = mq % 2
            if (g.cfg or {}).get("precast"):
                st("syn", lambda e, S, mq=mq, b=buf: e.dma_start(
                    out=WBs[b][:, :], in_=g.sc_winb[mq]),
                   reads=[g.bSCR["sc_winb"], g.bWB[buf]], writes=[g.bWB[buf]],
                   inc="lw", incn=16)
            else:
                for k in range(8):
                    st("gps", lambda e, S, k=k, mq=mq, b=buf: e.dma_start(
                        out=WBs[b][:, k * 1024:(k + 1) * 1024],
                        in_=dr["w_in"][k * 128:(k + 1) * 128,
                                       mq * 1024:(mq + 1) * 1024]),
                       reads=[g.bWB[buf]], writes=[g.bWB[buf]], inc="lw", incn=16)
            for mm in range(8):
                m = mq * 8 + mm
                bank = mm % 4

                def f(e, S, mm=mm, b=buf, bank=bank):
                    ins = None
                    for k in range(8):
                        ins = e.matmul(
                            g.PS[bank][:, :],
                            WBs[b][:, k * 1024 + mm * 128:k * 1024 + mm * 128 + 128],
                            g.I1T[:, k * GT:(k + 1) * GT],
                            start=(k == 0), stop=(k == 7))
                    return ins
                st("ten", f, reads=[g.bI1T, g.bWB[buf], g.bPS[bank]],
                   writes=[g.bPS[bank]], inc="tq")

                if grp == 0 and m < 16:
                    def f(e, S, mm=mm, b=buf, m=m):
                        ins = None
                        for k in range(8):
                            ins = e.matmul(
                                g.PS[5][:, m * 3:(m + 1) * 3],
                                WBs[b][:, k * 1024 + mm * 128:
                                       k * 1024 + mm * 128 + 128],
                                g.LN3B[:, k * 3:(k + 1) * 3],
                                start=(k == 0), stop=(k == 7))
                        return ins
                    st("ten", f, reads=[g.bLN3B, g.bWB[buf], g.bPS[5]],
                       writes=[g.bPS[5]], inc="tq")

                if m < 16:
                    st("sca", lambda e, S, m=m, bank=bank: e.activation(
                        out=g.UC[:, m * GT:(m + 1) * GT], in_=g.PS[bank][:, :],
                        func=AF.Identity, bias=g.BIN[:, m:m + 1]),
                       reads=[g.bPS[bank], g.bconst, g.bUC], writes=[g.bUC],
                       inc="cq")
                    if grp > 0:
                        emit_conv_chunk(g, grp, m)
                else:
                    st("sca", lambda e, S, m=m, bank=bank: e.activation(
                        out=g.SG[:, (m - 16) * GT:(m - 15) * GT],
                        in_=g.PS[bank][:, :], func=AF.Sigmoid,
                        bias=g.BIN[:, m:m + 1]),
                       reads=[g.bPS[bank], g.bconst, g.bSG], writes=[g.bSG],
                       inc="cq")

        if grp == 0:
            def f(e, S):
                ins = None
                for m in range(16):
                    ins = e.activation(out=g.MPS[:, m * 3:(m + 1) * 3],
                                       in_=g.PS[5][:, m * 3:(m + 1) * 3],
                                       func=AF.Identity, bias=g.BIN[:, m:m + 1])
                return ins
            st("sca", f, reads=[g.bPS[5], g.bconst, g.bMPS], writes=[g.bMPS],
               inc="cq")
            st("vec", lambda e, S: e.tensor_copy(out=g.UTL[:, :], in_=g.MPS[:, :]),
               reads=[g.bMPS, g.bUTL], writes=[g.bUTL], inc="vq")
            emit_exchange(g, "tail")
            if mstop == "winq":
                return

        if grp == 0:
            for m in range(16):
                emit_conv_chunk(g, grp, m)

        if mstop == "conv" and grp == 0:
            return
        for me in range(8):
            buf = 2 + (me % 2)
            if (g.cfg or {}).get("precast"):
                st("syn", lambda e, S, me=me, b=buf: e.dma_start(
                    out=WBs[b][:, :], in_=g.sc_wbcb[me]),
                   reads=[g.bSCR["sc_wbcb"], g.bWB[buf]], writes=[g.bWB[buf]],
                   inc="lw", incn=16)
            else:
                for kh in range(2):
                    st("gps", lambda e, S, me=me, kh=kh, b=buf: e.dma_start(
                        out=WBs[b].ap().rearrange("p (k c) -> p k c", k=16)
                            [:, kh * 8:(kh + 1) * 8, :],
                        in_=dr["w_bc"].rearrange("(k p) c -> p k c", p=128)
                            [:, kh * 8:(kh + 1) * 8, me * 512:(me + 1) * 512]),
                       reads=[g.bWB[buf]], writes=[g.bWB[buf]], inc="lw", incn=16)
            for mm in range(4):
                m = me * 4 + mm
                bank = mm % 4

                def f(e, S, mm=mm, b=buf, bank=bank):
                    ins = None
                    for k in range(16):
                        ins = e.matmul(
                            g.PS[bank][:, :],
                            WBs[b][:, k * 512 + mm * 128:k * 512 + mm * 128 + 128],
                            g.UC[:, k * GT:(k + 1) * GT],
                            start=(k == 0), stop=(k == 15))
                    return ins
                st("ten", f, reads=[g.bUC, g.bWB[buf], g.bPS[bank]],
                   writes=[g.bPS[bank]], inc="tq")

                if m < 16:
                    cb = m % 2
                    st("sca", lambda e, S, m=m, bank=bank, cb=cb: e.activation(
                        out=g.BCH[cb][:, :], in_=g.PS[bank][:, :], func=AF.Identity,
                        bias=g.BBC[:, m:m + 1]),
                       reads=[g.bPS[bank], g.bconst, g.bBCH[cb]],
                       writes=[g.bBCH[cb]], inc="cq")

                    def f(e, S, m=m, cb=cb, grp=grp):
                        a_b = g.ACOL[:, m:m + 1].broadcast_to([128, GT])
                        init = 0.0 if grp == 0 else g.STCOL[:, m:m + 1]
                        return e.tensor_tensor_scan(
                            out=g.STC[cb][:, :], data0=a_b, data1=g.BCH[cb][:, :],
                            initial=init, op0=ALU.mult, op1=ALU.add)
                    st("vec", f, reads=[g.bBCH[cb], g.bconst, g.bSTCOL,
                                        g.bSTC[cb]],
                       writes=[g.bSTC[cb]], inc="vq")
                    st("vec", lambda e, S, m=m, cb=cb: e.tensor_copy(
                        out=g.STCOL[:, m:m + 1], in_=g.STC[cb][:, GT - 1:GT]),
                       reads=[g.bSTC[cb], g.bSTCOL], writes=[g.bSTCOL], inc="vq")
                    st("syn", lambda e, S, m=m, cb=cb, grp=grp: e.dma_start(
                        out=g.sc_state[m * 128:(m + 1) * 128,
                                       grp * GT:(grp + 1) * GT],
                        in_=g.STC[cb][:, :]),
                       reads=[g.bSTC[cb]], writes=[g.bSCR["sc_state"]],
                       inc="sp", incn=16)
                else:
                    mc = m - 16
                    cb = m % 2
                    st("vec", lambda e, S, m=m, mc=mc, bank=bank, cb=cb:
                       e.scalar_tensor_tensor(
                           out=g.CGC[cb][:, :], in0=g.PS[bank][:, :],
                           scalar=g.BBC[:, m:m + 1],
                           in1=g.SG[:, mc * GT:(mc + 1) * GT],
                           op0=ALU.add, op1=ALU.mult),
                       reads=[g.bPS[bank], g.bconst, g.bSG, g.bCGC[cb]],
                       writes=[g.bCGC[cb]], inc="vq")
                    st("syn", lambda e, S, mc=mc, cb=cb, grp=grp: e.dma_start(
                        out=g.sc_cg[mc * 128:(mc + 1) * 128,
                                    grp * GT:(grp + 1) * GT],
                        in_=g.CGC[cb][:, :]),
                       reads=[g.bCGC[cb]], writes=[g.bSCR["sc_cg"]],
                       inc="sp", incn=16)

    if mstop == "nocarry":
        st("gps", lambda e, S: e.memset(g.CARR[:, :], 0.0),
           reads=[g.bSTCOL], writes=[g.bCARR], inc="uq")
        return
    emit_exchange(g, "carry")



def emit_phase3(g):
    st = g.step
    WBs = g.WBs
    dr = g.dr

    st("syn", lambda e, S: e.dma_start(out=g.IOTA, in_=dr["iota_in"][:, :]),
       reads=[g.bS1], writes=[g.bS1], inc="lr", incn=16)
    st("syn", lambda e, S: e.dma_start(out=g.BOUTB[:, :], in_=dr["b_out_bc"][:, :]),
       reads=[g.bS2], writes=[g.bS2], inc="lr", incn=16)

    if (g.cfg or {}).get("precast"):
        for half in range(2):
            st("syn", lambda e, S, h=half: e.dma_start(
                out=WBs[h][:, :], in_=g.sc_woutb[:, h * 8192:(h + 1) * 8192]),
               reads=[g.bSCR["sc_woutb"], g.bWB[half]], writes=[g.bWB[half]],
               inc="lw", incn=16)
    else:
        for k in range(16):
            buf = k // 8
            st("gps", lambda e, S, k=k, b=buf: e.dma_start(
                out=WBs[b][:, (k % 8) * 1024:(k % 8) * 1024 + 1024],
                in_=dr["w_out"][k * 128:(k + 1) * 128, :]),
               reads=[g.bWB[buf]], writes=[g.bWB[buf]], inc="lw", incn=16)

    for grp in range(NG):
        for o in range(16):
            cb = o % 2
            st("syn", lambda e, S, o=o, cb=cb, grp=grp: e.dma_start(
                out=g.STC[cb][:, :],
                in_=g.sc_state[o * 128:(o + 1) * 128, grp * GT:(grp + 1) * GT]),
               reads=[g.bSCR["sc_state"], g.bSTC[cb]], writes=[g.bSTC[cb]],
               inc="lr", incn=16)
            st("syn", lambda e, S, o=o, cb=cb, grp=grp: e.dma_start(
                out=g.CGC[cb][:, :],
                in_=g.sc_cg[o * 128:(o + 1) * 128, grp * GT:(grp + 1) * GT]),
               reads=[g.bSCR["sc_cg"], g.bCGC[cb]], writes=[g.bCGC[cb]],
               inc="lr", incn=16)
            st("vec", lambda e, S, o=o, grp=grp: e.tensor_scalar_mul(
                out=g.STATS[:, 11:12], in0=g.SPC[:, o:o + 1],
                scalar1=-float(grp * GT + 1)),
               reads=[g.bconst, g.bSTATS], writes=[g.bSTATS], inc="vq")
            st("sca", lambda e, S, o=o: e.activation(
                out=g.APOW, in_=g.IOTA, func=AF.Exp,
                scale=g.NSP[:, o:o + 1], bias=g.STATS[:, 11:12]),
               reads=[g.bconst, g.bSTATS, g.bS1], writes=[g.bS1], inc="cq")
            st("vec", lambda e, S, o=o, cb=cb: e.scalar_tensor_tensor(
                out=g.STC[cb][:, :], in0=g.APOW,
                scalar=g.CARR[:, o:o + 1], in1=g.STC[cb][:, :],
                op0=ALU.mult, op1=ALU.add),
               reads=[g.bS1, g.bCARR, g.bSTC[cb]], writes=[g.bSTC[cb]],
               inc="vq")
            st("vec", lambda e, S, cb=cb: e.tensor_mul(
                out=g.YB[cb][:, :], in0=g.STC[cb][:, :], in1=g.CGC[cb][:, :]),
               reads=[g.bSTC[cb], g.bCGC[cb], g.bYB[cb]], writes=[g.bYB[cb]],
               inc="vq")
            def f(e, S, o=o, cb=cb):
                ins = None
                for sg in range(SPG):
                    for n2 in range(2):
                        bank = sg * 2 + n2
                        ins = e.matmul(
                            g.PS[bank][:, :],
                            g.YB[cb][:, sg * 128:(sg + 1) * 128],
                            WBs[o // 8][:, (o % 8) * 1024 + n2 * 512:
                                        (o % 8) * 1024 + n2 * 512 + 512],
                            start=(o == 0), stop=(o == 15))
                return ins
            rds = [g.bYB[cb], g.bWB[0], g.bWB[1]] + \
                  ([b for b in g.bPS] if o == 0 else [])
            st("ten", f, reads=rds, writes=g.bPS if o == 15 else [], inc="tq")

        for sg in range(SPG):
            s = grp * SPG + sg
            for n2 in range(2):
                bank = sg * 2 + n2

                def f(e, S, s=s, n2=n2, bank=bank):
                    e.tensor_add(out=g.BCH[0][:, :],
                                 in0=g.PS[bank][:, :],
                                 in1=g.BOUTB[:, n2 * 512:(n2 + 1) * 512])
                    return e.tensor_add(
                        out=g.X[:, s * D + n2 * 512:s * D + n2 * 512 + 512],
                        in0=g.X[:, s * D + n2 * 512:s * D + n2 * 512 + 512],
                        in1=g.BCH[0][:, :])
                st("vec", f, reads=[g.bPS[bank], g.bS2, g.bBCH[0], g.bXs[s]],
                   writes=[g.bXs[s], g.bBCH[0]], inc="vq")


def emit_store(g):
    st = g.step
    for grp in range(NG):
        st("syn", lambda e, S, grp=grp: e.dma_start(
            out=g.y_out.rearrange("(s p) f -> p s f", p=128)
                [:, grp * SPG:(grp + 1) * SPG, :],
            in_=g.X.ap().rearrange("p (s f) -> p s f", s=NSLAB)
                [:, grp * SPG:(grp + 1) * SPG, :]),
           reads=[g.bXs[grp * SPG + i] for i in range(SPG)], inc="so", incn=16)
    tgt = g.w.cnt("qs")
    g.w.emit("syn", lambda e, S, t=tgt: e.wait_ge(S["qs"], t))
    g.w.emit("gps", lambda e, S, t=tgt: e.wait_ge(S["qs"], t))



def kernel(L0_attn_x, L0_attn_W_in, L0_attn_b_in, L0_attn_conv_w, L0_attn_conv_b,
           L0_attn_W_bc, L0_attn_b_bc, L0_attn_log_decay, L0_attn_W_out,
           L0_attn_b_out, L0_ffn_w_gate, L0_ffn_w_up, L0_ffn_w_down, _cfg=None):
    x = np.ascontiguousarray(np.asarray(L0_attn_x, np.float32))
    xf = x.reshape(B * N, D)
    b_out = np.asarray(L0_attn_b_out, np.float32)
    common = {
        "w_in": np.ascontiguousarray(np.asarray(L0_attn_W_in, np.float32)),
        "b_in": np.ascontiguousarray(np.asarray(L0_attn_b_in, np.float32)),
        "conv_w": np.ascontiguousarray(np.asarray(L0_attn_conv_w, np.float32)[:, 0, :]),
        "conv_b": np.ascontiguousarray(np.asarray(L0_attn_conv_b, np.float32)),
        "w_bc": np.ascontiguousarray(np.asarray(L0_attn_W_bc, np.float32)),
        "b_bc": np.ascontiguousarray(np.asarray(L0_attn_b_bc, np.float32)),
        "log_dec": np.ascontiguousarray(np.asarray(L0_attn_log_decay, np.float32)),
        "w_out": np.ascontiguousarray(np.asarray(L0_attn_W_out, np.float32)),
        "b_out_bc": np.ascontiguousarray(np.tile(b_out[None, :], (128, 1))),
        "wg": np.ascontiguousarray(np.asarray(L0_ffn_w_gate, np.float32)),
        "wu": np.ascontiguousarray(np.asarray(L0_ffn_w_up, np.float32)),
        "wd": np.ascontiguousarray(np.asarray(L0_ffn_w_down, np.float32)),
        "ident_in": np.eye(128, dtype=ml_dtypes.bfloat16),
        "iota_in": np.ascontiguousarray(
            np.tile(np.arange(GT, dtype=np.float32)[None, :], (128, 1))),
        "nonce_in": np.random.randint(1, 2**31 - 1, size=(1, 2)).astype(np.int32),
    }
    in_maps = []
    for ci in range(8):
        m = dict(common)
        m["x_in"] = np.ascontiguousarray(xf[ci * TC:(ci + 1) * TC])
        m["parity_in"] = np.array([[ci % 2, 0]], np.int32)
        in_maps.append(m)

    cfg = dict(_cfg or {})
    cfg.setdefault("precast", True)
    nc = build_nc(cfg)
    res = run_bass_kernel_spmd(nc, in_maps, core_ids=list(range(8)))
    out = np.concatenate([res.results[ci]["y_out"] for ci in range(8)], axis=0)
    return out.reshape(B, N, D)


# version 12 (final)
# speedup vs baseline: 1.0434x; 1.0434x over previous
import numpy as np
import ml_dtypes
import concourse.bass as bass
import concourse.mybir as mybir
from concourse.bass_utils import run_bass_kernel_spmd

F32 = mybir.dt.float32
BF16 = mybir.dt.bfloat16
I32 = mybir.dt.int32
AF = mybir.ActivationFunctionType
ALU = mybir.AluOpType
AX = mybir.AxisListType

MAGIC = float(np.float32(3 * 2**22))

D = 1024
DI = 2048
B, N = 4, 4096
TC = 2048
NSLAB = 16
NG = 4
SPG = 4
GT = 512
INNER = 4096
KC = 4

ENGINES = ("ten", "vec", "sca", "syn", "gps")
DMA_SEMS = ("lc", "lq", "sq", "lw", "sp", "so", "lr")
QUEUE_SEM = {"gps": "qg", "syn": "qs", "sca": "qa"}
ENG_SEMS = ("tq", "vq", "cq", "gq", "uq", "qg", "qs", "qa")
BRANCH_SEMS = ("exw", "exd", "exp")


class Walk:
    def __init__(self):
        self.steps = {e: [] for e in ENGINES}
        self.led = {}

    def emit(self, eng, fn):
        self.steps[eng].append(fn)

    def bump(self, sem, n):
        c = self.led.get(sem, 0) + n
        self.led[sem] = c
        return c

    def cnt(self, sem):
        return self.led.get(sem, 0)


class Buf:
    def __init__(self, name=""):
        self.name = name
        self.rd = []
        self.wr = []

    def note_write(self, dep):
        self.rd = [dep]
        self.wr = []

    def note_read(self, dep):
        self.wr.append(dep)


def merge_deps(deps, cur_eng):
    best = {}
    for eng, sem, cnt in deps:
        if eng == cur_eng:
            continue
        if sem is None:
            continue
        best[sem] = max(best.get(sem, 0), cnt)
    return sorted(best.items())


class Ctx:
    def __init__(self, walk):
        self.w = walk

    def step(self, eng, build, reads=(), writes=(), inc=None, incn=1):
        w = self.w
        deps = []
        for b in reads:
            deps += b.rd
        for b in writes:
            deps += b.wr + b.rd
        waits = merge_deps(deps, eng)
        if inc is not None and inc in DMA_SEMS:
            inc = QUEUE_SEM[eng]
        if inc is not None:
            tgt = w.bump(inc, incn)
            dep = (("dma" if inc in ("qg", "qs", "qa") else eng), inc, tgt)
        else:
            tgt = None
            dep = None

        drain_eng = eng in ("vec", "sca", "gps") and inc not in ("qg", "qs", "qa")

        def f(e, S, _w=waits, _b=build, _i=inc, _n=incn, _d=drain_eng):
            for s, v in _w:
                e.wait_ge(S[s], v)
            ins = _b(e, S)
            if _i is not None:
                assert ins is not None
                if _d:
                    ins = e.drain()
                ins.then_inc(S[_i], _n)

        w.emit(eng, f)
        if dep is not None:
            for b in reads:
                b.note_read(dep)
            for b in writes:
                b.note_write(dep)
        return dep



def build_nc(cfg=None):
    cfg = cfg or {}
    dump = cfg.get("dump")

    nc = bass.Bass("TRN2", target_bir_lowering=False)

    dr = {}
    for nm, shape, dt in [
        ("x_in", [TC, D], F32), ("w_in", [D, 2 * DI], F32), ("b_in", [2 * DI], F32),
        ("conv_w", [DI, KC], F32), ("conv_b", [DI], F32), ("w_bc", [DI, 2 * DI], F32),
        ("b_bc", [2 * DI], F32), ("log_dec", [DI], F32), ("w_out", [DI, D], F32),
        ("b_out_bc", [128, D], F32), ("wg", [INNER, D], F32), ("wu", [INNER, D], F32),
        ("wd", [D, INNER], F32), ("ident_in", [128, 128], BF16),
        ("iota_in", [128, GT], F32), ("parity_in", [1, 2], I32),
        ("nonce_in", [1, 2], I32),
    ]:
        dr[nm] = nc.dram_tensor(nm, shape, dt, kind="ExternalInput").ap()
    y_out = nc.dram_tensor("y_out", [TC, D], F32, kind="ExternalOutput").ap()

    sc_g = nc.dram_tensor("sc_g", [INNER, D], BF16, kind="Internal").ap()
    sc_u = nc.dram_tensor("sc_u", [INNER, D], BF16, kind="Internal").ap()
    sc_d = nc.dram_tensor("sc_d", [D, INNER], BF16, kind="Internal").ap()
    sc_dt = nc.dram_tensor("sc_dt", [INNER, D], BF16, kind="Internal").ap()
    sc_gtb = nc.dram_tensor("sc_gtb", [128, 4 * 8192], BF16, kind="Internal").ap()
    sc_utb = nc.dram_tensor("sc_utb", [128, 4 * 8192], BF16, kind="Internal").ap()
    sc_winb = nc.dram_tensor("sc_winb", [4, 128, 8192], BF16, kind="Internal").ap()
    sc_wbcb = nc.dram_tensor("sc_wbcb", [8, 128, 8192], BF16, kind="Internal").ap()
    sc_woutb = nc.dram_tensor("sc_woutb", [128, 16384], BF16, kind="Internal").ap()
    sc_state = nc.dram_tensor("sc_state", [16 * 128, TC], F32, kind="Internal").ap()
    sc_cg = nc.dram_tensor("sc_cg", [16 * 128, TC], F32, kind="Internal").ap()
    shared = nc.dram_tensor("shared", [128, 80], F32, kind="Internal",
                            addr_space="Shared").ap()

    ctxs = []

    def sbuf(name, shape, dt):
        t = nc.sbuf_tensor(name, shape, dt)
        ctxs.append(t)
        return t.__enter__()

    def psum(name, shape):
        t = nc.psum_tensor(name, shape, F32)
        ctxs.append(t)
        return t.__enter__()

    X = sbuf("X", [128, NSLAB * D], F32)
    WBA = sbuf("WBA", [128, 8192], BF16)
    WBB = sbuf("WBB", [128, 8192], BF16)
    WBC = sbuf("WBC", [128, 8192], BF16)
    WBD = sbuf("WBD", [128, 8192], BF16)
    GS = sbuf("GS", [128, SPG * INNER], BF16)
    I1T = sbuf("I1T", [128, 8 * GT], BF16)
    I2Ta = sbuf("I2Ta", [128, INNER], BF16)
    I2Tb = I2Ta
    I1 = sbuf("I1", [128, D], BF16)
    S1 = sbuf("S1", [128, 1024], F32)
    S2 = sbuf("S2", [128, 1024], F32)
    STATS = sbuf("STATS", [128, 16], F32)
    CONSTV = sbuf("CONSTV", [128, 2], F32)
    nc.const_aps.register(F32, MAGIC, CONSTV[:, 0:1])
    nc.const_aps.register(F32, 1.0, CONSTV[:, 1:2])
    UC = GS[:, 0:8192]
    SG = GS[:, 8192:16384]
    UCH = [sbuf(f"UCH{i}", [128, GT + KC], BF16) for i in range(2)]
    CVB = [sbuf(f"CVB{i}", [128, GT], BF16) for i in range(2)]
    BCH = [sbuf(f"BCH{i}", [128, GT], F32) for i in range(2)]
    STC = [sbuf(f"STC{i}", [128, GT], F32) for i in range(2)]
    CGC = [sbuf(f"CGC{i}", [128, GT], F32) for i in range(2)]
    YB = [sbuf(f"YB{i}", [128, GT], BF16) for i in range(2)]
    TAILS = sbuf("TAILS", [128, 2 * 16 * (KC - 1)], BF16)
    STCOL = sbuf("STCOL", [128, 16], F32)
    IDN = sbuf("IDN", [128, 128], BF16)
    ONES = sbuf("ONES", [128, 128], F32)
    BIN = sbuf("BIN", [128, 32], F32)
    BBC = sbuf("BBC", [128, 32], F32)
    CW = sbuf("CW", [128, 16 * KC], F32)
    CB = sbuf("CB", [128, 16], F32)
    LDC = sbuf("LDC", [128, 16], F32)
    ACOL = sbuf("ACOL", [128, 16], F32)
    SPC = sbuf("SPC", [128, 16], F32)
    NSP = sbuf("NSP", [128, 16], F32)
    WS = sbuf("WS", [128, 8], F32)
    GSC = sbuf("GSC", [128, 16], F32)
    USC = sbuf("USC", [128, 16], F32)
    DSC = sbuf("DSC", [128, 16], F32)
    IOTA = S1[:, 0:GT]
    APOW = S1[:, GT:2 * GT]
    BOUTB = S2
    PRT = sbuf("PRT", [1, 2], I32)
    NONC = sbuf("NONC", [1, 2], I32)
    FLG = sbuf("FLG", [1, 2], I32)
    CARR = sbuf("CARR", [128, 16], F32)
    LN3B = sbuf("LN3B", [128, 24], BF16)
    UTL = sbuf("UTL", [128, 48], BF16)
    MPS = sbuf("MPS", [128, 48], F32)
    ACCS = sbuf("ACCS", [128, 8], F32)

    PS = [psum(f"ps{i}", [128, 512]) for i in range(8)]

    sems = {}
    for s in DMA_SEMS + ENG_SEMS + BRANCH_SEMS:
        h = nc.semaphore(s)
        ctxs.append(h)
        sems[s] = h.__enter__()

    blk_cm = nc.Block()
    blk = blk_cm.__enter__()

    w = Walk()
    c = Ctx(w)

    env = dict(locals())
    g = EnvNS(env)
    emit_all(g)

    @blk.tensor
    def _(e):
        for f in w.steps["ten"]:
            f(e, sems)

    @blk.vector
    def _(e):
        for f in w.steps["vec"]:
            f(e, sems)

    @blk.scalar
    def _(e):
        for f in w.steps["sca"]:
            f(e, sems)

    @blk.sync
    def _(e):
        for f in w.steps["syn"]:
            f(e, sems)

    @blk.gpsimd
    def _(e):
        for f in w.steps["gps"]:
            f(e, sems)

    blk_cm.__exit__(None, None, None)
    for t in reversed(ctxs):
        t.__exit__(None, None, None)
    return nc


class EnvNS:
    def __init__(self, env):
        self.cfg = env.get("cfg")
        self.__dict__.update(env)
        self.__dict__.update(env["dr"])
        c = env["c"]
        self.step = c.step
        self.w = env["w"]
        for nm in ("GS I1T I1 I2 S1 S2 STATS TAILS STCOL WS GSC "
                   "USC DSC CARR LN3B UTL MPS ACCS").split():
            setattr(self, "b" + nm, Buf(nm))
        self.bXs = [Buf(f"X{s}") for s in range(NSLAB)]
        self.S0 = env["S2"]
        self.bS0 = self.bS2
        self.bUC = self.bGS
        self.bSG = self.bGS
        self.bAPOW = self.bS1
        self.bWB = [Buf(f"WB{i}") for i in range(4)]
        b2 = Buf("I2T")
        self.bI2T = [b2, b2]
        self.I2T = [env["I2Ta"], env["I2Ta"]]
        self.bUCH = [Buf(f"UCH{i}") for i in range(2)]
        self.bCVB = [Buf(f"CVB{i}") for i in range(2)]
        self.bBCH = [Buf(f"BCH{i}") for i in range(2)]
        self.bSTC = [Buf(f"STC{i}") for i in range(2)]
        self.bCGC = [Buf(f"CGC{i}") for i in range(2)]
        self.bYB = [Buf(f"YB{i}") for i in range(2)]
        self.bPS = [Buf(f"ps{i}") for i in range(8)]
        self.bconst = Buf("const")
        self.bSCR = {n: Buf(n) for n in
                     ("sc_g", "sc_u", "sc_d", "sc_dt", "sc_state", "sc_cg",
                      "sc_winb", "sc_wbcb", "sc_woutb")}
        self.bSCRgt = {"sc_gtb": [Buf(f"gtb{i}") for i in range(4)],
                       "sc_utb": [Buf(f"utb{i}") for i in range(4)]}
        self.bQstg = [Buf(f"qstg{i}") for i in range(8)]
        self.WBs = [env["WBA"], env["WBB"], env["WBC"], env["WBD"]]
        self.ex_w = 0
        self.ex_d = 0



def emit_all(g):
    emit_startup(g)
    dump = g.dump
    if dump == "x0":
        emit_store(g)
        return
    if dump == "dbgw":
        st = g.step
        for i, scr in enumerate([g.sc_g, g.sc_u]):
            st("gps", lambda e, S, i=i, s=scr: e.dma_start(
                out=g.X[:, i * D:(i + 1) * D], in_=s[0:128, :]),
               reads=[g.bSCR["sc_g"], g.bSCR["sc_u"]], writes=[g.bXs[0]],
               inc="lc", incn=16)
        st("gps", lambda e, S: e.dma_start(
            out=g.X.ap().rearrange("p (a c) -> p a c", a=NSLAB)[:, 2:6, :],
            in_=g.sc_d[0:128, :].rearrange("p (a c) -> p a c", a=4)),
           reads=[g.bSCR["sc_d"]], writes=[g.bXs[0]], inc="lc", incn=16)
        st("gps", lambda e, S: e.dma_start(
            out=g.X.ap().rearrange("p (a c) -> p a c", a=NSLAB)[:, 6:10, :],
            in_=g.sc_dt[0:512, :].rearrange("(a p) c -> p a c", p=128)),
           reads=[g.bSCR["sc_dt"]], writes=[g.bXs[0]], inc="lc", incn=16)
        st("vec", lambda e, S: e.tensor_copy(out=g.X[:, 10 * D:10 * D + 8],
                                             in_=g.WS[:, :]),
           reads=[g.bWS], writes=[g.bXs[0]], inc="vq")
        emit_store(g)
        return
    if dump == "dbgex":
        st = g.step
        st("vec", lambda e, S: e.memset(g.UTL[:, :], 3.0),
           writes=[g.bUTL], inc="vq")
        emit_exchange(g, "tail")
        st("vec", lambda e, S: e.tensor_copy(out=g.X[:, 1 * D:1 * D + 48],
                                             in_=g.TAILS[:, 0:48]),
           reads=[g.bTAILS], writes=[g.bXs[0]], inc="vq")
        emit_store(g)
        return
    if dump == "exf1":
        st = g.step
        emit_ffn(g, "f1")
        emit_mini_prep(g)
        st("vec", lambda e, S: e.memset(g.UTL[:, :], 3.0),
           writes=[g.bUTL], inc="vq")
        emit_exchange(g, "tail")
        st("vec", lambda e, S: e.tensor_copy(out=g.X[:, 0:48],
                                             in_=g.TAILS[:, 0:48]),
           reads=[g.bTAILS], writes=[g.bXs[0]], inc="vq")
        emit_store(g)
        return
    if dump == "dbgex2":
        st = g.step
        st("vec", lambda e, S: e.memset(g.UTL[:, :], 3.0),
           writes=[g.bUTL], inc="vq")
        st("vec", lambda e, S: e.memset(g.STCOL[:, :], 5.0),
           writes=[g.bSTCOL], inc="vq")
        emit_exchange(g, "tail")
        emit_exchange(g, "carry")
        def f(e, S):
            e.tensor_copy(out=g.X[:, 1 * D:1 * D + 48], in_=g.TAILS[:, 0:48])
            return e.tensor_copy(out=g.X[:, 2 * D:2 * D + 16], in_=g.CARR[:, :])
        st("vec", f, reads=[g.bTAILS, g.bCARR], writes=[g.bXs[0]], inc="vq")
        emit_store(g)
        return
    if dump == "dbgt":
        st = g.step
        st("gps", lambda e, S: e.dma_start(
            out=g.X.ap().rearrange("p (q f) -> p q f", q=4),
            in_=g.sc_gtb.rearrange("p (q f) -> p q f", q=4)[:, :, 0:4096]),
           reads=[g.bSCRgt["sc_gtb"][0], g.bSCRgt["sc_gtb"][1],
                  g.bSCRgt["sc_gtb"][2], g.bSCRgt["sc_gtb"][3]],
           writes=[g.bXs[0]], inc="lc", incn=16)
        emit_store(g)
        return
    if dump == "dbgq1":
        st = g.step
        emit_ln_q_common(g, 0, None, True)
        st("vec", lambda e, S: e.tensor_copy(out=g.X[:, 1 * D:2 * D],
                                             in_=g.I1[:, :]),
           reads=[g.bI1], writes=[g.bXs[0]], inc="vq")
        st("vec", lambda e, S: e.tensor_copy(out=g.X[:, 2 * D:3 * D],
                                             in_=g.S1[:, :]),
           reads=[g.bS1], writes=[g.bXs[0]], inc="vq")
        st("vec", lambda e, S: e.tensor_copy(out=g.X[:, 3 * D:4 * D],
                                             in_=g.S2[:, :]),
           reads=[g.bS2], writes=[g.bXs[0]], inc="vq")

        def f(e, S):
            e.tensor_copy(out=g.X[:, 4 * D:4 * D + 16], in_=g.STATS[:, :])
            e.tensor_copy(out=g.X[:, 4 * D + 16:4 * D + 32], in_=g.GSC[:, :])
            return e.tensor_copy(out=g.X[:, 4 * D + 32:4 * D + 48], in_=g.USC[:, :])
        st("vec", f, reads=[g.bSTATS, g.bGSC, g.bUSC], writes=[g.bXs[0]], inc="vq")
        emit_store(g)
        return
    emit_ffn(g, "f1")
    if (g.cfg or {}).get("precast"):
        emit_precast(g)
    emit_mini_prep(g)
    if dump == "x1":
        emit_store(g)
        return
    emit_mamba(g)
    if (g.cfg or {}).get("mstop") in ("winq", "conv", "g0", "g1", "g2"):
        emit_store(g)
        return
    emit_phase3(g)
    if dump == "x2":
        emit_store(g)
        return
    emit_ffn(g, "f2")
    emit_store(g)



def emit_startup(g):
    st = g.step
    dr = g.dr
    X = g.X

    def f(e, S):
        e.memset(g.CONSTV[:, 0:1], MAGIC)
        return e.memset(g.CONSTV[:, 1:2], 1.0)
    st("vec", f, writes=[g.bconst], inc="vq")

    st("gps", lambda e, S: e.dma_start(
        out=X.ap().rearrange("p (s f) -> p s f", s=NSLAB),
        in_=dr["x_in"].rearrange("(s p) f -> p s f", p=128)),
       writes=g.bXs, inc="lc", incn=16)

    def ld(out_ap, in_ap):
        st("gps", lambda e, S, o=out_ap, i=in_ap: e.dma_start(
            out=o, in_=i, allow_slow_non_contiguous=True),
           writes=[g.bconst], inc="lc", incn=16)

    ld(g.IDN[:, :], dr["ident_in"][:, :])
    ld(g.PRT[:, :], dr["parity_in"][:, :])
    ld(g.NONC[:, :], dr["nonce_in"][:, :])
    ld(g.BIN[:, :], dr["b_in"].rearrange("(m p) -> p m", p=128))
    ld(g.BBC[:, :], dr["b_bc"].rearrange("(m p) -> p m", p=128))
    ld(g.CW.ap().rearrange("p (o k) -> p o k", o=16),
       dr["conv_w"].rearrange("(o p) k -> p o k", p=128))
    ld(g.CB[:, :], dr["conv_b"].rearrange("(o p) -> p o", p=128))
    ld(g.LDC[:, :], dr["log_dec"].rearrange("(o p) -> p o", p=128))

    st("vec", lambda e, S: e.memset(g.ONES[:, :], 1.0),
       writes=[g.bconst], inc="vq")

    def f(e, S):
        e.activation(out=g.S0[:, 0:16], in_=g.LDC[:, :], func=AF.Abs)
        e.drain()
        e.activation(out=g.S0[:, 16:32], in_=g.S0[:, 0:16], func=AF.Exp, scale=-1.0)
        e.drain()
        e.activation(out=g.S0[:, 32:48], in_=g.S0[:, 16:32], func=AF.Ln, bias=1.0)
        return e.activation(out=g.S0[:, 48:64], in_=g.LDC[:, :], func=AF.Relu)
    st("sca", f, reads=[g.bconst], writes=[g.bS0], inc="cq")

    def f(e, S):
        e.tensor_add(out=g.SPC[:, :], in0=g.S0[:, 32:48], in1=g.S0[:, 48:64])
        e.drain()
        return e.tensor_scalar_mul(out=g.NSP[:, :], in0=g.SPC[:, :], scalar1=-1.0)
    st("vec", f, reads=[g.bS0], writes=[g.bconst], inc="vq")

    st("sca", lambda e, S: e.activation(out=g.ACOL[:, :], in_=g.SPC[:, :],
                                        func=AF.Exp, scale=-1.0),
       reads=[g.bconst], writes=[g.bconst], inc="cq")

    QW = [g.GS.ap().bitcast(F32)[:, 0:4096], g.GS.ap().bitcast(F32)[:, 4096:8192]]
    QT = [g.WBs[0][:, 0:4096], g.WBs[0][:, 4096:8192]]
    bQTh = [Buf("qt0"), Buf("qt1")]
    bUCF2 = Buf("ucf")
    UCF = g.WBs[1].ap().bitcast(F32)
    bQW = g.bGS
    bUCF = bUCF2
    scr_names = ["sc_g", "sc_u", "sc_d"]

    early_q1 = not (g.cfg or {}).get("no_early_q1")
    if early_q1:
        for _sg in range(SPG):
            emit_ln_q_common(g, _sg, None, True, raw_scales=True)
            emit_i1_transpose(g, _sg)

    mats = [(dr["wd"], g.sc_d, "sc_d", 8, 1, 2), (dr["wg"], g.sc_g, "sc_g", 8, 4, 0),
            (dr["wu"], g.sc_u, "sc_u", 8, 4, 1)]
    bACCs = [Buf(f"acc{i}") for i in range(3)]
    bQWh = [Buf("qw0"), Buf("qw1")]
    g.bSCRq = {"sc_g": [Buf(f"sc_g{i}") for i in range(4)],
               "sc_u": [Buf(f"sc_u{i}") for i in range(4)]}

    def ld_super(e, wap, cb, b, nsub):
        if nsub == 1:
            return e.dma_start(out=QW[b][:, 0:4096],
                               in_=wap[cb * 128:(cb + 1) * 128, :])
        return e.dma_start(
            out=QW[b].rearrange("p (a f) -> p a f", a=4),
            in_=wap[cb * 512:(cb + 1) * 512, :].rearrange("(a p) f -> p a f", p=128))

    if early_q1:
        pass

    for mi, (wap, scr, snm, nsc, nsub, wcol) in enumerate(mats):
        for cb in range(nsc):
            bb = cb % 2
            st("gps", lambda e, S, a=wap, cb=cb, b=bb, ns=nsub: ld_super(e, a, cb, b, ns),
               reads=[bQWh[bb]], writes=[bQWh[bb]], inc="lq", incn=16)
            if cb == 0:
                st("vec", lambda e, S, b=bb, mi=mi: e.reduce_sum(
                    out=g.ACCS[:, 2 * mi:2 * mi + 1], in_=QW[b][:, 0:4096],
                    axis=AX.X, apply_absolute_value=True),
                   reads=[bQWh[bb]], writes=[bACCs[mi]], inc="vq")
            else:
                def f(e, S, b=bb, mi=mi):
                    e.reduce_sum(out=g.ACCS[:, 6:7], in_=QW[b][:, 0:4096],
                                 axis=AX.X, apply_absolute_value=True)
                    e.drain()
                    return e.tensor_add(out=g.ACCS[:, 2 * mi:2 * mi + 1],
                                        in0=g.ACCS[:, 2 * mi:2 * mi + 1],
                                        in1=g.ACCS[:, 6:7])
                st("vec", f, reads=[bQWh[bb], bACCs[mi]], writes=[bACCs[mi]],
                   inc="vq")

        st("ten", lambda e, S, mi=mi: e.matmul(
            g.PS[4][:, 0:1], g.ONES[:, 0:128], g.ACCS[:, 2 * mi:2 * mi + 1],
            start=True, stop=True),
           reads=[bACCs[mi], g.bconst, g.bPS[4]], writes=[g.bPS[4]], inc="tq")

        def f(e, S, m=wcol):
            e.tensor_scalar(out=g.WS[:, m:m + 1], in0=g.PS[4][:, 0:1],
                            scalar1=1.0 / 4194304.0, scalar2=1e-5,
                            op0=ALU.mult, op1=ALU.max)
            e.drain()
            return e.reciprocal(out=g.WS[:, 4 + m:5 + m], in_=g.WS[:, m:m + 1])
        st("vec", f, reads=[g.bPS[4]], writes=[g.bWS], inc="vq")

        tscr = {"sc_g": g.sc_gtb, "sc_u": g.sc_utb}.get(snm)
        tbufs = {"sc_g": g.bSCRgt["sc_gtb"], "sc_u": g.bSCRgt["sc_utb"]}.get(snm)
        for cb in range(nsc):
            bb = cb % 2
            st("gps" if cb % 2 == 0 else "syn",
               lambda e, S, a=wap, cb=cb, b=bb, ns=nsub: ld_super(e, a, cb, b, ns),
               reads=[bQWh[bb]], writes=[bQWh[bb]], inc="lq", incn=16)

            st("sca", lambda e, S, b=bb, m=wcol: e.activation(
                out=UCF[:, 0:4096], in_=QW[b][:, 0:4096], func=AF.Identity,
                scale=g.WS[:, 4 + m:5 + m], bias=MAGIC),
               reads=[bQWh[bb], g.bWS, bUCF], writes=[bUCF], inc="cq")

            def f(e, S, b=bb):
                e.tensor_scalar(out=UCF[:, 0:4096], in0=UCF[:, 0:4096],
                                scalar1=MAGIC, scalar2=-1.0,
                                op0=ALU.subtract, op1=ALU.max)
                return e.tensor_scalar(out=QT[b][:, 0:4096], in0=UCF[:, 0:4096],
                                       scalar1=1.0, scalar2=None, op0=ALU.min)
            st("vec", f, reads=[g.bWS, bUCF, bQTh[bb]],
               writes=[bQTh[bb], bUCF], inc="vq")

            if nsub == 1:
                st("sca", lambda e, S, s=scr, cb=cb, b=bb: e.dma_start(
                    out=s[cb * 128:(cb + 1) * 128, :], in_=QT[b][:, 0:4096]),
                   reads=[bQTh[bb]], writes=[g.bSCR[snm]], inc="sq", incn=16)
                continue

            qq = cb // 2
            for fk in range(8):
                bank = 6 + (fk % 2)

                def f(e, S, b=bb, fk=fk, bank=bank):
                    ins = None
                    for a in range(4):
                        ins = e.matmul(
                            g.PS[bank][:, a * 128:(a + 1) * 128],
                            QT[b][:, a * 1024 + fk * 128:a * 1024 + fk * 128 + 128],
                            g.IDN[:, :], start=True, stop=True)
                    return ins
                st("ten", f, reads=[bQTh[bb], g.bconst, g.bPS[bank]],
                   writes=[g.bPS[bank]], inc="tq")
                stg = g.I2T[0][:, fk * 512:(fk + 1) * 512]
                bstg = g.bI2T[0] if False else g.bQstg[fk]
                st("sca", lambda e, S, bank=bank, stg=stg: e.activation(
                    out=stg, in_=g.PS[bank][:, :], func=AF.Identity),
                   reads=[g.bPS[bank], bstg], writes=[bstg], inc="cq")
                st("sca", lambda e, S, t=tscr, qq=qq, cb=cb, fk=fk, stg=stg:
                   e.dma_start(
                       out=t[:, qq * 8192 + fk * 1024 + (cb % 2) * 512:
                              qq * 8192 + fk * 1024 + (cb % 2) * 512 + 512],
                       in_=stg),
                   reads=[bstg], writes=[tbufs[qq]], inc="sq", incn=16)

        if early_q1 and snm == "sc_u":
            def f(e, S):
                e.tensor_mul(out=g.USC[:, 0:4], in0=g.GSC[:, 0:4],
                             in1=g.WS[:, 1:2].broadcast_to([128, 4]))
                e.drain()
                return e.tensor_mul(out=g.GSC[:, 0:4], in0=g.GSC[:, 0:4],
                                    in1=g.WS[:, 0:1].broadcast_to([128, 4]))
            st("vec", f, reads=[g.bGSC, g.bWS], writes=[g.bGSC, g.bUSC], inc="vq")

    for kt in range(32):
        bb = kt % 2
        st("syn", lambda e, S, k=kt, b=bb: e.dma_start(
            out=QT[b][:, 0:1024], in_=g.sc_d[:, k * 128:(k + 1) * 128],
            transpose=True),
           reads=[g.bSCR["sc_d"], bQTh[bb]], writes=[bQTh[bb]],
           inc="lw", incn=16)
        st("sca", lambda e, S, k=kt, b=bb: e.dma_start(
            out=g.sc_dt[k * 128:(k + 1) * 128, :], in_=QT[b][:, 0:1024]),
           reads=[bQTh[bb]], writes=[g.bSCR["sc_dt"]], inc="sp", incn=16)

    for bq in g.bQstg:
        for dep in bq.wr + bq.rd:
            g.bI2T[0].note_read(dep)
    for dep in bQTh[0].wr + bQTh[1].wr + bQTh[0].rd + bQTh[1].rd:
        g.bWB[0].note_read(dep)
    for dep in bUCF2.wr + bUCF2.rd:
        g.bWB[1].note_read(dep)



def emit_ln_q_common(g, s, out_bf, want_quant, raw_scales=False):
    st = g.step
    Xs = g.X[:, s * D:(s + 1) * D]

    bXs = g.bXs[s]
    st("vec", lambda e, S: e.reduce_sum(out=g.STATS[:, 0:1], in_=Xs, axis=AX.X),
       reads=[bXs, g.bSTATS], writes=[g.bSTATS], inc="vq")
    st("sca", lambda e, S: e.activation(out=g.S0[:, :], in_=Xs, func=AF.Square,
                                        accum_out=g.STATS[:, 1:2]),
       reads=[bXs, g.bS0, g.bSTATS], writes=[g.bS0, g.bSTATS], inc="cq")

    def f(e, S):
        e.tensor_scalar_mul(out=g.STATS[:, 2:3], in0=g.STATS[:, 0:1],
                            scalar1=1.0 / D)
        e.drain()
        e.tensor_mul(out=g.STATS[:, 3:4], in0=g.STATS[:, 2:3], in1=g.STATS[:, 2:3])
        e.drain()
        e.scalar_tensor_tensor(out=g.STATS[:, 4:5], in0=g.STATS[:, 1:2],
                               scalar=1.0 / D, in1=g.STATS[:, 3:4],
                               op0=ALU.mult, op1=ALU.subtract)
        e.drain()
        return e.tensor_scalar_add(out=g.STATS[:, 4:5], in0=g.STATS[:, 4:5],
                                   scalar1=1e-5)
    st("vec", f, reads=[g.bSTATS], writes=[g.bSTATS], inc="vq")
    st("sca", lambda e, S: e.activation(out=g.STATS[:, 12:13], in_=g.STATS[:, 4:5],
                                        func=AF.Sqrt),
       reads=[g.bSTATS], writes=[g.bSTATS], inc="cq")

    def f(e, S):
        e.reciprocal(out=g.STATS[:, 5:6], in_=g.STATS[:, 12:13])
        e.drain()
        return e.scalar_tensor_tensor(
            out=g.STATS[:, 6:7], in0=g.STATS[:, 2:3], scalar=-1.0,
            in1=g.STATS[:, 5:6], op0=ALU.mult, op1=ALU.mult)
    st("vec", f, reads=[g.bSTATS], writes=[g.bSTATS], inc="vq")

    if not want_quant:
        st("sca", lambda e, S: e.activation(out=g.I1[:, :], in_=Xs, func=AF.Identity,
                                            scale=g.STATS[:, 5:6],
                                            bias=g.STATS[:, 6:7]),
           reads=[bXs, g.bSTATS, g.bI1], writes=[g.bI1], inc="cq")
        return

    st("sca", lambda e, S: e.activation(out=g.S1[:, :], in_=Xs, func=AF.Identity,
                                        scale=g.STATS[:, 5:6], bias=g.STATS[:, 6:7]),
       reads=[bXs, g.bSTATS, g.bS1], writes=[g.bS1], inc="cq")

    def f(e, S):
        e.reduce_max(out=g.STATS[:, 7:8], in_=g.S1[:, :], axis=AX.X,
                     apply_absolute_value=True)
        e.drain()
        e.tensor_scalar_max(out=g.STATS[:, 8:9], in0=g.STATS[:, 7:8], scalar1=1e-5)
        e.drain()
        e.reciprocal(out=g.STATS[:, 9:10], in_=g.STATS[:, 8:9])
        e.drain()
        return e.tensor_scalar_mul(out=g.STATS[:, 10:11], in0=g.STATS[:, 9:10],
                                   scalar1=127.0)
    st("vec", f, reads=[g.bS1, g.bSTATS], writes=[g.bSTATS], inc="vq")

    st("sca", lambda e, S: e.activation(out=g.S2[:, :], in_=g.S1[:, :],
                                        func=AF.Identity, scale=g.STATS[:, 10:11],
                                        bias=MAGIC),
       reads=[g.bS1, g.bSTATS, g.bS2], writes=[g.bS2], inc="cq")
    st("vec", lambda e, S: e.tensor_scalar(out=g.I1[:, :], in0=g.S2[:, :],
                                           scalar1=MAGIC, scalar2=None,
                                           op0=ALU.subtract),
       reads=[g.bS2, g.bI1], writes=[g.bI1], inc="vq")

    if raw_scales:
        st("vec", lambda e, S, s=s: e.tensor_scalar_mul(
            out=g.GSC[:, s % 16:s % 16 + 1], in0=g.STATS[:, 8:9],
            scalar1=1.0 / 127.0),
           reads=[g.bSTATS], writes=[g.bGSC], inc="vq")
        return

    def f(e, S, s=s):
        e.scalar_tensor_tensor(out=g.GSC[:, s % 16:s % 16 + 1],
                               in0=g.STATS[:, 8:9], scalar=1.0 / 127.0,
                               in1=g.WS[:, 0:1], op0=ALU.mult, op1=ALU.mult)
        return e.scalar_tensor_tensor(out=g.USC[:, s % 16:s % 16 + 1],
                                      in0=g.STATS[:, 8:9], scalar=1.0 / 127.0,
                                      in1=g.WS[:, 1:2], op0=ALU.mult, op1=ALU.mult)
    st("vec", f, reads=[g.bSTATS, g.bWS], writes=[g.bGSC, g.bUSC], inc="vq")


def emit_i1_transpose(g, sg):
    st = g.step
    for half in range(2):
        bank = 4 + half

        def f(e, S, h=half, bank=bank):
            ins = None
            for j in range(4):
                k = h * 4 + j
                ins = e.matmul(g.PS[bank][:, j * 128:(j + 1) * 128],
                               g.I1[:, k * 128:(k + 1) * 128], g.IDN[:, :],
                               start=True, stop=True)
            return ins
        st("ten", f, reads=[g.bI1, g.bconst, g.bPS[bank]], writes=[g.bPS[bank]],
           inc="tq")

        def f(e, S, h=half, bank=bank, sg=sg):
            ins = None
            for j in range(4):
                k = h * 4 + j
                ins = e.tensor_copy(
                    out=g.I1T[:, k * GT + sg * 128:k * GT + sg * 128 + 128],
                    in_=g.PS[bank][:, j * 128:(j + 1) * 128])
            return ins
        st("vec", f, reads=[g.bPS[bank], g.bI1T], writes=[g.bI1T], inc="vq")



def emit_gu_old(g, sg, q, gb, ub, grp):
    st = g.step
    WBs = g.WBs
    for n2 in range(2):
        bank = (sg % 2) * 2 + n2

        def fmm(e, S, sg=sg, n2=n2, b=gb, bank=bank):
            ins = None
            for k in range(8):
                ins = e.matmul(
                    g.PS[bank][:, :],
                    g.I1T[:, k * GT + sg * 128:k * GT + sg * 128 + 128],
                    WBs[b][:, k * 1024 + n2 * 512:k * 1024 + n2 * 512 + 512],
                    start=(k == 0), stop=(k == 7))
            return ins
        st("ten", fmm, reads=[g.bI1T, g.bWB[gb], g.bPS[bank]],
           writes=[g.bPS[bank]], inc="tq")
        cbase = sg * INNER + q * 1024 + n2 * 512
        scol = (grp * SPG + sg) % 16
        st("sca", lambda e, S, cb=cbase, bank=bank, sc=scol: e.activation(
            out=g.GS[:, cb:cb + 512], in_=g.PS[bank][:, :], func=AF.Silu,
            scale=g.GSC[:, sc:sc + 1]),
           reads=[g.bPS[bank], g.bGSC, g.bGS], writes=[g.bGS], inc="cq")

        def fmm(e, S, sg=sg, n2=n2, b=ub, bank=bank):
            ins = None
            for k in range(8):
                ins = e.matmul(
                    g.PS[bank][:, :],
                    g.I1T[:, k * GT + sg * 128:k * GT + sg * 128 + 128],
                    WBs[b][:, k * 1024 + n2 * 512:k * 1024 + n2 * 512 + 512],
                    start=(k == 0), stop=(k == 7))
            return ins
        st("ten", fmm, reads=[g.bI1T, g.bWB[ub], g.bPS[bank]],
           writes=[g.bPS[bank]], inc="tq")
        st("vec", lambda e, S, cb=cbase, bank=bank, sc=scol: e.scalar_tensor_tensor(
            out=g.GS[:, cb:cb + 512], in0=g.PS[bank][:, :],
            scalar=g.USC[:, sc:sc + 1], in1=g.GS[:, cb:cb + 512],
            op0=ALU.mult, op1=ALU.mult),
           reads=[g.bPS[bank], g.bUSC, g.bGS], writes=[g.bGS], inc="vq")


def emit_ffn(g, tag):
    st = g.step
    WBs = g.WBs
    cfg = g.cfg or {}
    ngr = cfg.get(tag + "_groups", NG)
    q1only = cfg.get(tag + "_q1only", False)

    for grp in range(ngr):
        if not (tag == "f1" and grp == 0
                and not (g.cfg or {}).get("no_early_q1")):
            for sg in range(SPG):
                emit_ln_q_common(g, grp * SPG + sg, None, True)
                emit_i1_transpose(g, sg)
        if q1only:
            continue

        for q in range(4):
            gb, ub = (q % 2) * 2, (q % 2) * 2 + 1
            for buf, tn in ((gb, "sc_gtb"), (ub, "sc_utb")):
                tsc = g.sc_gtb if tn == "sc_gtb" else g.sc_utb
                bscr = g.bSCRgt[tn][q]
                st("syn", lambda e, S, b=buf, t=tsc, q=q: e.dma_start(
                    out=WBs[b][:, :], in_=t[:, q * 8192:(q + 1) * 8192]),
                   reads=[bscr, g.bWB[buf]], writes=[g.bWB[buf]],
                   inc="lw", incn=16)
            for sg in range(SPG):
                if (g.cfg or {}).get("old_gu"):
                    emit_gu_old(g, sg, q, gb, ub, grp)
                    continue
                b0 = 4 * (sg % 2)
                bks = [b0, b0 + 1, b0 + 2, b0 + 3]

                def fmm(e, S, sg=sg, gb=gb, ub=ub, b0=b0):
                    ins = None
                    for k in range(8):
                        lhs = g.I1T[:, k * GT + sg * 128:k * GT + sg * 128 + 128]
                        for j, b in ((0, gb), (1, gb), (2, ub), (3, ub)):
                            n2 = j % 2
                            ins = e.matmul(
                                g.PS[b0 + j][:, :], lhs,
                                WBs[b][:, k * 1024 + n2 * 512:
                                       k * 1024 + n2 * 512 + 512],
                                start=(k == 0), stop=(k == 7))
                    return ins
                st("ten", fmm,
                   reads=[g.bI1T, g.bWB[gb], g.bWB[ub]] + [g.bPS[b] for b in bks],
                   writes=[g.bPS[b] for b in bks], inc="tq")

                scol = (grp * SPG + sg) % 16
                for n2 in range(2):
                    cbase = sg * INNER + q * 1024 + n2 * 512
                    st("sca", lambda e, S, cb=cbase, bank=b0 + n2, sc=scol:
                       e.activation(
                           out=g.GS[:, cb:cb + 512],
                           in_=g.PS[bank][:, :], func=AF.Silu,
                           scale=g.GSC[:, sc:sc + 1]),
                       reads=[g.bPS[b0 + n2], g.bGSC, g.bGS], writes=[g.bGS],
                       inc="cq")
                for n2 in range(2):
                    cbase = sg * INNER + q * 1024 + n2 * 512
                    st("vec", lambda e, S, cb=cbase, bank=b0 + 2 + n2, sc=scol:
                       e.scalar_tensor_tensor(
                           out=g.GS[:, cb:cb + 512],
                           in0=g.PS[bank][:, :],
                           scalar=g.USC[:, sc:sc + 1],
                           in1=g.GS[:, cb:cb + 512],
                           op0=ALU.mult, op1=ALU.mult),
                       reads=[g.bPS[b0 + 2 + n2], g.bUSC, g.bGS], writes=[g.bGS],
                       inc="vq")

        for kt in range(32):
            buf = kt // 8
            st("syn", lambda e, S, k=kt, b=buf: e.dma_start(
                out=WBs[b][:, (k % 8) * 1024:(k % 8) * 1024 + 1024],
                in_=g.sc_dt[k * 128:(k + 1) * 128, :]),
               reads=[g.bSCR["sc_dt"], g.bWB[buf]], writes=[g.bWB[buf]],
               inc="lw", incn=16)

        for sg in range(SPG):
            s = grp * SPG + sg
            hs = g.GS[:, sg * INNER:(sg + 1) * INNER]

            def f(e, S, hs=hs):
                e.reduce_max(out=g.STATS[:, 7:8], in_=hs, axis=AX.X,
                             apply_absolute_value=True)
                e.drain()
                e.tensor_scalar_max(out=g.STATS[:, 8:9], in0=g.STATS[:, 7:8],
                                    scalar1=1e-5)
                e.drain()
                e.reciprocal(out=g.STATS[:, 9:10], in_=g.STATS[:, 8:9])
                e.drain()
                return e.tensor_scalar_mul(out=g.STATS[:, 10:11],
                                           in0=g.STATS[:, 9:10], scalar1=127.0)
            st("vec", f, reads=[g.bGS, g.bSTATS], writes=[g.bSTATS], inc="vq")
            st("vec", lambda e, S, s=s: e.scalar_tensor_tensor(
                out=g.DSC[:, s % 16:s % 16 + 1], in0=g.STATS[:, 8:9],
                scalar=0.5 / 127.0, in1=g.WS[:, 2:3], op0=ALU.mult, op1=ALU.mult),
               reads=[g.bSTATS, g.bWS], writes=[g.bDSC], inc="vq")

            tb = 0
            for qq in range(4):
                st("sca", lambda e, S, hs=hs, qq=qq: e.activation(
                    out=g.S2[:, :], in_=hs[:, qq * 1024:(qq + 1) * 1024],
                    func=AF.Identity, scale=g.STATS[:, 10:11], bias=MAGIC),
                   reads=[g.bGS, g.bSTATS, g.bS2], writes=[g.bS2], inc="cq")
                st("vec", lambda e, S: e.tensor_scalar(
                    out=g.I1[:, :], in0=g.S2[:, :],
                    scalar1=MAGIC, scalar2=None, op0=ALU.subtract),
                   reads=[g.bS2, g.bI1], writes=[g.bI1], inc="vq")
                for r in range(2):
                    bank = 4 + (r % 2)

                    def f(e, S, r=r, bank=bank):
                        ins = None
                        for j in range(4):
                            k = r * 4 + j
                            ins = e.matmul(g.PS[bank][:, j * 128:(j + 1) * 128],
                                           g.I1[:, k * 128:(k + 1) * 128],
                                           g.IDN[:, :], start=True, stop=True)
                        return ins
                    st("ten", f, reads=[g.bI1, g.bconst, g.bPS[bank]],
                       writes=[g.bPS[bank]], inc="tq")
                    st("vec", lambda e, S, r=r, bank=bank, qq=qq: e.tensor_copy(
                        out=g.I2T[0][:, qq * 1024 + r * 512:
                                    qq * 1024 + r * 512 + 512],
                        in_=g.PS[bank][:, :]),
                       reads=[g.bPS[bank], g.bI2T[0]], writes=[g.bI2T[0]],
                       inc="vq")

            for n2 in range(2):
                bank = 6 + n2

                def f(e, S, n2=n2, bank=bank, tb=tb):
                    ins = None
                    for k in range(32):
                        ins = e.matmul(
                            g.PS[bank][:, :],
                            g.I2T[tb][:, k * 128:(k + 1) * 128],
                            WBs[k // 8][:, (k % 8) * 1024 + n2 * 512:
                                        (k % 8) * 1024 + n2 * 512 + 512],
                            start=(k == 0), stop=(k == 31))
                    return ins
                st("ten", f, reads=[g.bI2T[tb], g.bWB[0], g.bWB[1], g.bWB[2],
                                    g.bWB[3], g.bPS[bank]],
                   writes=[g.bPS[bank]], inc="tq")
                st("vec", lambda e, S, s=s, n2=n2, bank=bank:
                   e.scalar_tensor_tensor(
                       out=g.X[:, s * D + n2 * 512:s * D + n2 * 512 + 512],
                       in0=g.PS[bank][:, :], scalar=g.DSC[:, s % 16:s % 16 + 1],
                       in1=g.X[:, s * D + n2 * 512:s * D + n2 * 512 + 512],
                       op0=ALU.mult, op1=ALU.add),
                   reads=[g.bPS[bank], g.bDSC, g.bXs[s]], writes=[g.bXs[s]],
                   inc="vq")



def emit_precast(g):
    st = g.step
    dr = g.dr
    for mq in range(4):
        st("gps", lambda e, S, mq=mq: e.dma_start(
            out=g.sc_winb[mq].rearrange("p (k c) -> p k c", k=8),
            in_=dr["w_in"].rearrange("(k p) c -> p k c", p=128)
                [:, :, mq * 1024:(mq + 1) * 1024]),
           writes=[g.bSCR["sc_winb"]], inc="sp", incn=16)
    for me in range(8):
        st("gps", lambda e, S, me=me: e.dma_start(
            out=g.sc_wbcb[me].rearrange("p (k c) -> p k c", k=16),
            in_=dr["w_bc"].rearrange("(k p) c -> p k c", p=128)
                [:, :, me * 512:(me + 1) * 512]),
           writes=[g.bSCR["sc_wbcb"]], inc="sp", incn=16)
    st("gps", lambda e, S: e.dma_start(
        out=g.sc_woutb.rearrange("p (k c) -> p k c", k=16),
        in_=dr["w_out"].rearrange("(k p) c -> p k c", p=128)),
       writes=[g.bSCR["sc_woutb"]], inc="sp", incn=16)


def emit_mini_prep(g):
    st = g.step
    st("gps", lambda e, S: e.dma_start(out=g.S1[0:3, :],
                                       in_=g.X[125:128, 15 * D:16 * D]),
       reads=[g.bXs[15], g.bS1], writes=[g.bS1], inc="lc", incn=16)

    def f(e, S):
        e.reduce_sum(out=g.STATS[0:3, 0:1], in_=g.S1[0:3, :], axis=AX.X)
        e.drain()
        return e.tensor_scalar_mul(out=g.STATS[0:3, 2:3], in0=g.STATS[0:3, 0:1],
                                   scalar1=1.0 / D)
    st("vec", f, reads=[g.bS1, g.bSTATS], writes=[g.bSTATS], inc="vq")
    st("sca", lambda e, S: e.activation(out=g.S0[0:3, :], in_=g.S1[0:3, :],
                                        func=AF.Square,
                                        accum_out=g.STATS[0:3, 1:2]),
       reads=[g.bS1, g.bS0, g.bSTATS], writes=[g.bS0, g.bSTATS], inc="cq")

    def f(e, S):
        e.tensor_mul(out=g.STATS[0:3, 3:4], in0=g.STATS[0:3, 2:3],
                     in1=g.STATS[0:3, 2:3])
        e.drain()
        e.scalar_tensor_tensor(out=g.STATS[0:3, 4:5],
                               in0=g.STATS[0:3, 1:2], scalar=1.0 / D,
                               in1=g.STATS[0:3, 3:4],
                               op0=ALU.mult, op1=ALU.subtract)
        e.drain()
        return e.tensor_scalar_add(out=g.STATS[0:3, 4:5],
                                   in0=g.STATS[0:3, 4:5], scalar1=1e-5)
    st("vec", f, reads=[g.bSTATS], writes=[g.bSTATS], inc="vq")
    st("sca", lambda e, S: e.activation(out=g.STATS[0:3, 12:13],
                                        in_=g.STATS[0:3, 4:5],
                                        func=AF.Sqrt),
       reads=[g.bSTATS], writes=[g.bSTATS], inc="cq")

    def f(e, S):
        e.reciprocal(out=g.STATS[0:3, 5:6], in_=g.STATS[0:3, 12:13])
        e.drain()
        return e.scalar_tensor_tensor(
            out=g.STATS[0:3, 6:7], in0=g.STATS[0:3, 2:3], scalar=-1.0,
            in1=g.STATS[0:3, 5:6], op0=ALU.mult, op1=ALU.mult)
    st("vec", f, reads=[g.bSTATS], writes=[g.bSTATS], inc="vq")
    st("sca", lambda e, S: e.activation(out=g.I1[0:3, :], in_=g.S1[0:3, :],
                                        func=AF.Identity, scale=g.STATS[0:3, 5:6],
                                        bias=g.STATS[0:3, 6:7]),
       reads=[g.bS1, g.bSTATS, g.bI1], writes=[g.bI1], inc="cq")

    for half in range(2):
        bank = 4 + half

        def f(e, S, h=half, bank=bank):
            ins = None
            for j in range(4):
                k = h * 4 + j
                ins = e.matmul(g.PS[bank][:, j * 3:(j + 1) * 3],
                               g.I1[0:3, k * 128:(k + 1) * 128], g.IDN[0:3, 0:3],
                               start=True, stop=True)
            return ins
        st("ten", f, reads=[g.bI1, g.bconst, g.bPS[bank]], writes=[g.bPS[bank]],
           inc="tq")

        def f(e, S, h=half, bank=bank):
            ins = None
            for j in range(4):
                k = h * 4 + j
                ins = e.tensor_copy(out=g.LN3B[:, k * 3:(k + 1) * 3],
                                    in_=g.PS[bank][:, j * 3:(j + 1) * 3])
            return ins
        st("vec", f, reads=[g.bPS[bank], g.bLN3B], writes=[g.bLN3B], inc="vq")



def emit_exchange(g, what):
    st = g.step
    exv = (g.cfg or {}).get("exv", 3)
    if g.dump == "noex" or (g.cfg or {}).get("noex"):
        if what == "tail":
            st("gps", lambda e, S: e.memset(g.TAILS[:, 0:48], 0.0),
               reads=[g.bUTL], writes=[g.bTAILS], inc="uq")
        else:
            st("gps", lambda e, S: e.memset(g.CARR[:, :], 0.0),
               reads=[g.bSTCOL], writes=[g.bCARR], inc="uq")
        return
    shared = g.shared
    sharedB = shared.bitcast(BF16)
    sharedI = shared.bitcast(I32)
    fcol = 78 if what == "tail" else 79
    ncol = 0 if what == "tail" else 1

    if what == "tail":
        bsrc, bdst = g.bUTL, g.bTAILS
        w_out = lambda e: e.dma_start(out=sharedB[:, 32:80], in_=g.UTL[:, :])
        r_in = lambda e: e.dma_start(out=g.TAILS[:, 0:48], in_=sharedB[:, 32:80])
        zero = lambda e: e.memset(g.TAILS[:, 0:48], 0.0)
    else:
        bsrc, bdst = g.bSTCOL, g.bCARR
        w_out = lambda e: e.dma_start(out=shared[:, 0:16], in_=g.STCOL[:, :])
        r_in = lambda e: e.dma_start(out=g.CARR[:, :], in_=shared[:, 0:16])
        zero = lambda e: e.memset(g.CARR[:, :], 0.0)

    exw0 = g.ex_w
    exd0 = g.ex_d
    g.ex_w += 32
    g.ex_d += 16

    def f(e, S, exw0=exw0, exd0=exd0, exv=exv):
        rp = e.alloc_register()
        e.reg_load(rp, g.PRT[0:1, 0:1])
        rn = e.alloc_register()
        e.reg_load(rn, g.NONC[0:1, ncol:ncol + 1])
        with e.If_eq(rp, 0):
            if exv >= 2:
                w_out(e).then_inc(S["exw"], 16)
                e.wait_ge(S["exw"], exw0 + 16)
            e.dma_start(out=sharedI[0:1, fcol:fcol + 1],
                        in_=g.NONC[0:1, ncol:ncol + 1]).then_inc(S["exw"], 16)
            e.wait_ge(S["exw"], exw0 + 32 if exv >= 2 else exw0 + 16)
            zero(e)
        e.end_ifs()
        with e.If_eq(rp, 1):
            rf = e.alloc_register()
            rd = e.alloc_register()
            e.reg_mov(rd, 1)
            with e.While(rd):
                e.dma_start(out=g.FLG[0:1, ncol:ncol + 1],
                            in_=sharedI[0:1, fcol:fcol + 1]).then_inc(S["exp"], 16)
                e.nop(cycle_cnt=2500)
                e.reg_load(rf, g.FLG[0:1, ncol:ncol + 1])
                e.reg_sub(rd, rf, rn)
            if exv >= 2:
                r_in(e).then_inc(S["exd"], 16)
                e.wait_ge(S["exd"], exd0 + 16)
            else:
                zero(e)
        e.end_ifs()
        return e.engine_nop()

    st("gps", f, reads=[bsrc, g.bconst], writes=[bdst], inc="uq")



def emit_conv_chunk(g, grp, m):
    st = g.step
    tsl_rd = (grp % 2) * 48
    tsl_wr = ((grp + 1) % 2) * 48
    cb = m % 2

    def f(e, S, m=m, cb=cb, tr=tsl_rd, tw=tsl_wr):
        e.tensor_copy(out=g.TAILS[:, tw + m * 3:tw + (m + 1) * 3],
                      in_=g.UC[:, (m + 1) * GT - 3:(m + 1) * GT])
        e.tensor_copy(out=g.UCH[cb][:, 0:3],
                      in_=g.TAILS[:, tr + m * 3:tr + (m + 1) * 3])
        return e.tensor_copy(out=g.UCH[cb][:, 3:3 + GT],
                             in_=g.UC[:, m * GT:(m + 1) * GT])
    st("vec", f, reads=[g.bUC, g.bTAILS, g.bUCH[cb]],
       writes=[g.bUCH[cb], g.bTAILS], inc="vq")

    def f(e, S, m=m, cb=cb):
        e.tensor_scalar_mul(out=g.CVB[cb][:, :], in0=g.UCH[cb][:, 0:GT],
                            scalar1=g.CW[:, m * KC:m * KC + 1])
        ins = None
        for kk in range(1, KC):
            ins = e.scalar_tensor_tensor(
                out=g.CVB[cb][:, :], in0=g.UCH[cb][:, kk:kk + GT],
                scalar=g.CW[:, m * KC + kk:m * KC + kk + 1],
                in1=g.CVB[cb][:, :], op0=ALU.mult, op1=ALU.add)
        return ins
    st("vec", f, reads=[g.bUCH[cb], g.bconst, g.bCVB[cb]],
       writes=[g.bCVB[cb]], inc="vq")

    st("sca", lambda e, S, m=m, cb=cb: e.activation(
        out=g.UC[:, m * GT:(m + 1) * GT], in_=g.CVB[cb][:, :],
        func=AF.Silu, bias=g.CB[:, m:m + 1]),
       reads=[g.bCVB[cb], g.bconst, g.bUC], writes=[g.bUC], inc="cq")


def emit_mamba(g):
    st = g.step
    WBs = g.WBs
    dr = g.dr
    mstop = (g.cfg or {}).get("mstop")

    for grp in range(NG):
        if mstop in ("g1", "g2", "g3") and grp > int(mstop[1]):
            break
        for sg in range(SPG):
            emit_ln_q_common(g, grp * SPG + sg, None, False)
            emit_i1_transpose(g, sg)

        for mq in range(4):
            buf = mq % 2
            if (g.cfg or {}).get("precast"):
                st("syn", lambda e, S, mq=mq, b=buf: e.dma_start(
                    out=WBs[b][:, :], in_=g.sc_winb[mq]),
                   reads=[g.bSCR["sc_winb"], g.bWB[buf]], writes=[g.bWB[buf]],
                   inc="lw", incn=16)
            else:
                st("gps", lambda e, S, mq=mq, b=buf: e.dma_start(
                    out=WBs[b].ap().rearrange("p (k c) -> p k c", k=8),
                    in_=dr["w_in"].rearrange("(k p) c -> p k c", p=128)
                        [:, :, mq * 1024:(mq + 1) * 1024]),
                   reads=[g.bWB[buf]], writes=[g.bWB[buf]], inc="lw", incn=16)
            for mm in range(8):
                m = mq * 8 + mm
                bank = mm % 4

                def f(e, S, mm=mm, b=buf, bank=bank):
                    ins = None
                    for k in range(8):
                        ins = e.matmul(
                            g.PS[bank][:, :],
                            WBs[b][:, k * 1024 + mm * 128:k * 1024 + mm * 128 + 128],
                            g.I1T[:, k * GT:(k + 1) * GT],
                            start=(k == 0), stop=(k == 7))
                    return ins
                st("ten", f, reads=[g.bI1T, g.bWB[buf], g.bPS[bank]],
                   writes=[g.bPS[bank]], inc="tq")

                if grp == 0 and m < 16:
                    def f(e, S, mm=mm, b=buf, m=m):
                        ins = None
                        for k in range(8):
                            ins = e.matmul(
                                g.PS[5][:, m * 3:(m + 1) * 3],
                                WBs[b][:, k * 1024 + mm * 128:
                                       k * 1024 + mm * 128 + 128],
                                g.LN3B[:, k * 3:(k + 1) * 3],
                                start=(k == 0), stop=(k == 7))
                        return ins
                    st("ten", f, reads=[g.bLN3B, g.bWB[buf], g.bPS[5]],
                       writes=[g.bPS[5]], inc="tq")

                if m < 16:
                    st("sca", lambda e, S, m=m, bank=bank: e.activation(
                        out=g.UC[:, m * GT:(m + 1) * GT], in_=g.PS[bank][:, :],
                        func=AF.Identity, bias=g.BIN[:, m:m + 1]),
                       reads=[g.bPS[bank], g.bconst, g.bUC], writes=[g.bUC],
                       inc="cq")
                    if grp > 0:
                        emit_conv_chunk(g, grp, m)
                else:
                    st("sca", lambda e, S, m=m, bank=bank: e.activation(
                        out=g.SG[:, (m - 16) * GT:(m - 15) * GT],
                        in_=g.PS[bank][:, :], func=AF.Sigmoid,
                        bias=g.BIN[:, m:m + 1]),
                       reads=[g.bPS[bank], g.bconst, g.bSG], writes=[g.bSG],
                       inc="cq")

        if grp == 0:
            def f(e, S):
                ins = None
                for m in range(16):
                    ins = e.activation(out=g.MPS[:, m * 3:(m + 1) * 3],
                                       in_=g.PS[5][:, m * 3:(m + 1) * 3],
                                       func=AF.Identity, bias=g.BIN[:, m:m + 1])
                return ins
            st("sca", f, reads=[g.bPS[5], g.bconst, g.bMPS], writes=[g.bMPS],
               inc="cq")
            st("vec", lambda e, S: e.tensor_copy(out=g.UTL[:, :], in_=g.MPS[:, :]),
               reads=[g.bMPS, g.bUTL], writes=[g.bUTL], inc="vq")
            emit_exchange(g, "tail")
            if mstop == "winq":
                return

        if grp == 0:
            for m in range(16):
                emit_conv_chunk(g, grp, m)

        if mstop == "conv" and grp == 0:
            return
        for me in range(8):
            buf = 2 + (me % 2)
            if (g.cfg or {}).get("precast"):
                st("syn", lambda e, S, me=me, b=buf: e.dma_start(
                    out=WBs[b][:, :], in_=g.sc_wbcb[me]),
                   reads=[g.bSCR["sc_wbcb"], g.bWB[buf]], writes=[g.bWB[buf]],
                   inc="lw", incn=16)
            else:
                for kh in range(2):
                    st("gps", lambda e, S, me=me, kh=kh, b=buf: e.dma_start(
                        out=WBs[b].ap().rearrange("p (k c) -> p k c", k=16)
                            [:, kh * 8:(kh + 1) * 8, :],
                        in_=dr["w_bc"].rearrange("(k p) c -> p k c", p=128)
                            [:, kh * 8:(kh + 1) * 8, me * 512:(me + 1) * 512]),
                       reads=[g.bWB[buf]], writes=[g.bWB[buf]], inc="lw", incn=16)
            for mm in range(4):
                m = me * 4 + mm
                bank = mm % 4

                def f(e, S, mm=mm, b=buf, bank=bank):
                    ins = None
                    for k in range(16):
                        ins = e.matmul(
                            g.PS[bank][:, :],
                            WBs[b][:, k * 512 + mm * 128:k * 512 + mm * 128 + 128],
                            g.UC[:, k * GT:(k + 1) * GT],
                            start=(k == 0), stop=(k == 15))
                    return ins
                st("ten", f, reads=[g.bUC, g.bWB[buf], g.bPS[bank]],
                   writes=[g.bPS[bank]], inc="tq")

                if m < 16:
                    cb = m % 2
                    st("sca", lambda e, S, m=m, bank=bank, cb=cb: e.activation(
                        out=g.BCH[cb][:, :], in_=g.PS[bank][:, :], func=AF.Identity,
                        bias=g.BBC[:, m:m + 1]),
                       reads=[g.bPS[bank], g.bconst, g.bBCH[cb]],
                       writes=[g.bBCH[cb]], inc="cq")

                    def f(e, S, m=m, cb=cb, grp=grp):
                        a_b = g.ACOL[:, m:m + 1].broadcast_to([128, GT])
                        init = 0.0 if grp == 0 else g.STCOL[:, m:m + 1]
                        return e.tensor_tensor_scan(
                            out=g.STC[cb][:, :], data0=a_b, data1=g.BCH[cb][:, :],
                            initial=init, op0=ALU.mult, op1=ALU.add)
                    st("vec", f, reads=[g.bBCH[cb], g.bconst, g.bSTCOL,
                                        g.bSTC[cb]],
                       writes=[g.bSTC[cb]], inc="vq")
                    st("vec", lambda e, S, m=m, cb=cb: e.tensor_copy(
                        out=g.STCOL[:, m:m + 1], in_=g.STC[cb][:, GT - 1:GT]),
                       reads=[g.bSTC[cb], g.bSTCOL], writes=[g.bSTCOL], inc="vq")
                    st("syn", lambda e, S, m=m, cb=cb, grp=grp: e.dma_start(
                        out=g.sc_state[m * 128:(m + 1) * 128,
                                       grp * GT:(grp + 1) * GT],
                        in_=g.STC[cb][:, :]),
                       reads=[g.bSTC[cb]], writes=[g.bSCR["sc_state"]],
                       inc="sp", incn=16)
                else:
                    mc = m - 16
                    cb = m % 2
                    st("vec", lambda e, S, m=m, mc=mc, bank=bank, cb=cb:
                       e.scalar_tensor_tensor(
                           out=g.CGC[cb][:, :], in0=g.PS[bank][:, :],
                           scalar=g.BBC[:, m:m + 1],
                           in1=g.SG[:, mc * GT:(mc + 1) * GT],
                           op0=ALU.add, op1=ALU.mult),
                       reads=[g.bPS[bank], g.bconst, g.bSG, g.bCGC[cb]],
                       writes=[g.bCGC[cb]], inc="vq")
                    st("syn", lambda e, S, mc=mc, cb=cb, grp=grp: e.dma_start(
                        out=g.sc_cg[mc * 128:(mc + 1) * 128,
                                    grp * GT:(grp + 1) * GT],
                        in_=g.CGC[cb][:, :]),
                       reads=[g.bCGC[cb]], writes=[g.bSCR["sc_cg"]],
                       inc="sp", incn=16)

    if mstop == "nocarry":
        st("gps", lambda e, S: e.memset(g.CARR[:, :], 0.0),
           reads=[g.bSTCOL], writes=[g.bCARR], inc="uq")
        return
    emit_exchange(g, "carry")



def emit_phase3(g):
    st = g.step
    WBs = g.WBs
    dr = g.dr

    st("syn", lambda e, S: e.dma_start(out=g.IOTA, in_=dr["iota_in"][:, :]),
       reads=[g.bS1], writes=[g.bS1], inc="lr", incn=16)
    st("syn", lambda e, S: e.dma_start(out=g.BOUTB[:, :], in_=dr["b_out_bc"][:, :]),
       reads=[g.bS2], writes=[g.bS2], inc="lr", incn=16)

    if (g.cfg or {}).get("precast"):
        for half in range(2):
            st("syn", lambda e, S, h=half: e.dma_start(
                out=WBs[h][:, :], in_=g.sc_woutb[:, h * 8192:(h + 1) * 8192]),
               reads=[g.bSCR["sc_woutb"], g.bWB[half]], writes=[g.bWB[half]],
               inc="lw", incn=16)
    else:
        for k in range(16):
            buf = k // 8
            st("gps", lambda e, S, k=k, b=buf: e.dma_start(
                out=WBs[b][:, (k % 8) * 1024:(k % 8) * 1024 + 1024],
                in_=dr["w_out"][k * 128:(k + 1) * 128, :]),
               reads=[g.bWB[buf]], writes=[g.bWB[buf]], inc="lw", incn=16)

    for grp in range(NG):
        for o in range(16):
            cb = o % 2
            st("syn", lambda e, S, o=o, cb=cb, grp=grp: e.dma_start(
                out=g.STC[cb][:, :],
                in_=g.sc_state[o * 128:(o + 1) * 128, grp * GT:(grp + 1) * GT]),
               reads=[g.bSCR["sc_state"], g.bSTC[cb]], writes=[g.bSTC[cb]],
               inc="lr", incn=16)
            st("syn", lambda e, S, o=o, cb=cb, grp=grp: e.dma_start(
                out=g.CGC[cb][:, :],
                in_=g.sc_cg[o * 128:(o + 1) * 128, grp * GT:(grp + 1) * GT]),
               reads=[g.bSCR["sc_cg"], g.bCGC[cb]], writes=[g.bCGC[cb]],
               inc="lr", incn=16)
            st("vec", lambda e, S, o=o, grp=grp: e.tensor_scalar_mul(
                out=g.STATS[:, 11:12], in0=g.SPC[:, o:o + 1],
                scalar1=-float(grp * GT + 1)),
               reads=[g.bconst, g.bSTATS], writes=[g.bSTATS], inc="vq")
            st("sca", lambda e, S, o=o: e.activation(
                out=g.APOW, in_=g.IOTA, func=AF.Exp,
                scale=g.NSP[:, o:o + 1], bias=g.STATS[:, 11:12]),
               reads=[g.bconst, g.bSTATS, g.bS1], writes=[g.bS1], inc="cq")
            st("vec", lambda e, S, o=o, cb=cb: e.scalar_tensor_tensor(
                out=g.STC[cb][:, :], in0=g.APOW,
                scalar=g.CARR[:, o:o + 1], in1=g.STC[cb][:, :],
                op0=ALU.mult, op1=ALU.add),
               reads=[g.bS1, g.bCARR, g.bSTC[cb]], writes=[g.bSTC[cb]],
               inc="vq")
            st("vec", lambda e, S, cb=cb: e.tensor_mul(
                out=g.YB[cb][:, :], in0=g.STC[cb][:, :], in1=g.CGC[cb][:, :]),
               reads=[g.bSTC[cb], g.bCGC[cb], g.bYB[cb]], writes=[g.bYB[cb]],
               inc="vq")
            def f(e, S, o=o, cb=cb):
                ins = None
                for sg in range(SPG):
                    for n2 in range(2):
                        bank = sg * 2 + n2
                        ins = e.matmul(
                            g.PS[bank][:, :],
                            g.YB[cb][:, sg * 128:(sg + 1) * 128],
                            WBs[o // 8][:, (o % 8) * 1024 + n2 * 512:
                                        (o % 8) * 1024 + n2 * 512 + 512],
                            start=(o == 0), stop=(o == 15))
                return ins
            rds = [g.bYB[cb], g.bWB[0], g.bWB[1]] + \
                  ([b for b in g.bPS] if o == 0 else [])
            st("ten", f, reads=rds, writes=g.bPS if o == 15 else [], inc="tq")

        for sg in range(SPG):
            s = grp * SPG + sg
            for n2 in range(2):
                bank = sg * 2 + n2

                def f(e, S, s=s, n2=n2, bank=bank):
                    e.tensor_add(out=g.BCH[0][:, :],
                                 in0=g.PS[bank][:, :],
                                 in1=g.BOUTB[:, n2 * 512:(n2 + 1) * 512])
                    return e.tensor_add(
                        out=g.X[:, s * D + n2 * 512:s * D + n2 * 512 + 512],
                        in0=g.X[:, s * D + n2 * 512:s * D + n2 * 512 + 512],
                        in1=g.BCH[0][:, :])
                st("vec", f, reads=[g.bPS[bank], g.bS2, g.bBCH[0], g.bXs[s]],
                   writes=[g.bXs[s], g.bBCH[0]], inc="vq")


def emit_store(g):
    st = g.step
    for grp in range(NG):
        st("syn", lambda e, S, grp=grp: e.dma_start(
            out=g.y_out.rearrange("(s p) f -> p s f", p=128)
                [:, grp * SPG:(grp + 1) * SPG, :],
            in_=g.X.ap().rearrange("p (s f) -> p s f", s=NSLAB)
                [:, grp * SPG:(grp + 1) * SPG, :]),
           reads=[g.bXs[grp * SPG + i] for i in range(SPG)], inc="so", incn=16)
    tgt = g.w.cnt("qs")
    g.w.emit("syn", lambda e, S, t=tgt: e.wait_ge(S["qs"], t))
    g.w.emit("gps", lambda e, S, t=tgt: e.wait_ge(S["qs"], t))



def kernel(L0_attn_x, L0_attn_W_in, L0_attn_b_in, L0_attn_conv_w, L0_attn_conv_b,
           L0_attn_W_bc, L0_attn_b_bc, L0_attn_log_decay, L0_attn_W_out,
           L0_attn_b_out, L0_ffn_w_gate, L0_ffn_w_up, L0_ffn_w_down, _cfg=None):
    x = np.ascontiguousarray(np.asarray(L0_attn_x, np.float32))
    xf = x.reshape(B * N, D)
    b_out = np.asarray(L0_attn_b_out, np.float32)
    common = {
        "w_in": np.ascontiguousarray(np.asarray(L0_attn_W_in, np.float32)),
        "b_in": np.ascontiguousarray(np.asarray(L0_attn_b_in, np.float32)),
        "conv_w": np.ascontiguousarray(np.asarray(L0_attn_conv_w, np.float32)[:, 0, :]),
        "conv_b": np.ascontiguousarray(np.asarray(L0_attn_conv_b, np.float32)),
        "w_bc": np.ascontiguousarray(np.asarray(L0_attn_W_bc, np.float32)),
        "b_bc": np.ascontiguousarray(np.asarray(L0_attn_b_bc, np.float32)),
        "log_dec": np.ascontiguousarray(np.asarray(L0_attn_log_decay, np.float32)),
        "w_out": np.ascontiguousarray(np.asarray(L0_attn_W_out, np.float32)),
        "b_out_bc": np.ascontiguousarray(np.tile(b_out[None, :], (128, 1))),
        "wg": np.ascontiguousarray(np.asarray(L0_ffn_w_gate, np.float32)),
        "wu": np.ascontiguousarray(np.asarray(L0_ffn_w_up, np.float32)),
        "wd": np.ascontiguousarray(np.asarray(L0_ffn_w_down, np.float32)),
        "ident_in": np.eye(128, dtype=ml_dtypes.bfloat16),
        "iota_in": np.ascontiguousarray(
            np.tile(np.arange(GT, dtype=np.float32)[None, :], (128, 1))),
        "nonce_in": np.random.randint(1, 2**31 - 1, size=(1, 2)).astype(np.int32),
    }
    in_maps = []
    for ci in range(8):
        m = dict(common)
        m["x_in"] = np.ascontiguousarray(xf[ci * TC:(ci + 1) * TC])
        m["parity_in"] = np.array([[ci % 2, 0]], np.int32)
        in_maps.append(m)

    nc = build_nc(_cfg)
    res = run_bass_kernel_spmd(nc, in_maps, core_ids=list(range(8)))
    out = np.concatenate([res.results[ci]["y_out"] for ci in range(8)], axis=0)
    return out.reshape(B, N, D)
